# version 17
# baseline (speedup 1.0000x reference)
import contextlib
import numpy as np

import concourse.bass as bass
import concourse.mybir as mybir
from concourse.bass_utils import run_bass_kernel_spmd

F32 = mybir.dt.float32
BF16 = mybir.dt.bfloat16
AF = mybir.ActivationFunctionType
ALU = mybir.AluOpType

DM = 2048
NCH = 16
HEADS = 16
BLK = 256
INTER = 5632
ICH = 44
EPS_LN = 1e-5
EPS_RMS = 1e-6
MASK_NEG = -1.0e5


class Ledger:
    def __init__(self, nc, es, names):
        self.nc = nc
        self.h = {}
        self.c = {}
        self.lw = {}
        for n in names:
            self.h[n] = es.enter_context(nc.semaphore(n))
            self.c[n] = 0

    def inc(self, instr, name, k):
        instr.then_inc(self.h[name], k)
        self.c[name] += k
        return (name, self.c[name])

    def wait(self, eng, engname, *marks):
        for m in marks:
            if m is None:
                continue
            if isinstance(m, list):
                self.wait(eng, engname, *m)
                continue
            name, val = m
            if val <= 0:
                continue
            key = (engname, name)
            if self.lw.get(key, 0) >= val:
                continue
            eng.wait_ge(self.h[name], val)
            self.lw[key] = val

    def cur(self, name):
        return (name, self.c[name])


def build_program(n_dup=8, n_main=8):
    nc = bass.Bass("TRN2", target_bir_lowering=False, debug=False)

    xd_d = nc.dram_tensor("xd", [n_dup, NCH, 128, BLK], F32, kind="ExternalInput").ap()
    xm_d = nc.dram_tensor("xm", [n_main, NCH, 128, BLK], F32, kind="ExternalInput").ap()
    wq_d = nc.dram_tensor("wq", [64, 128, 2048], F32, kind="ExternalInput").ap()
    wo_d = nc.dram_tensor("wo", [16, 128, 2048], F32, kind="ExternalInput").ap()
    wg_d = nc.dram_tensor("wg", [44, 128, 2048], F32, kind="ExternalInput").ap()
    wu_d = nc.dram_tensor("wu", [44, 128, 2048], F32, kind="ExternalInput").ap()
    wd_d = nc.dram_tensor("wd", [16, 128, 5632], F32, kind="ExternalInput").ap()
    gs_d = nc.dram_tensor("gs", [128, 44], F32, kind="ExternalInput").ap()
    us_d = nc.dram_tensor("us", [128, 44], F32, kind="ExternalInput").ap()
    ds_d = nc.dram_tensor("ds", [128, 16], F32, kind="ExternalInput").ap()
    sl_d = nc.dram_tensor("sl", [128, 16], F32, kind="ExternalInput").ap()
    out_d = nc.dram_tensor("out", [n_main, NCH, 128, BLK], F32, kind="ExternalOutput").ap()

    wql_d = nc.dram_tensor("wql", [64, 128, 4096], BF16, kind="Internal").ap()
    wol_d = nc.dram_tensor("wol", [16, 128, 4096], BF16, kind="Internal").ap()
    wgb_d = nc.dram_tensor("wgb", [44, 128, 2048], BF16, kind="Internal").ap()
    wub_d = nc.dram_tensor("wub", [44, 128, 2048], BF16, kind="Internal").ap()
    wdb_d = nc.dram_tensor("wdb", [16, 128, 5632], BF16, kind="Internal").ap()

    import ml_dtypes
    ident_d = nc.inline_tensor(np.eye(128, dtype=np.float32), name="c_ident").ap()
    ones_col_d = nc.inline_tensor(np.ones((128, 1), np.float32), name="c_onesc").ap()
    ones_row_d = nc.inline_tensor(np.ones((1, 128), np.float32), name="c_onesr").ap()
    mi = np.full((128, 512), MASK_NEG, np.float32)
    for j in range(2):
        s = (np.arange(128) + j * 128)[:, None]
        m = np.arange(256)[None, :]
        v = (s - m).astype(np.float32)
        mi[:, j * 256:(j + 1) * 256] = np.where(m >= s, v, MASK_NEG)
    miota_d = nc.inline_tensor(mi.astype(ml_dtypes.bfloat16), name="c_miota").ap()
    negm1 = (-(np.arange(256, dtype=np.float32) + 1.0)).reshape(1, 256)
    negk = (np.arange(256, dtype=np.float32) - 255.0).reshape(1, 256)
    negm1_d = nc.inline_tensor(negm1.astype(ml_dtypes.bfloat16), name="c_negm1").ap()
    negk_d = nc.inline_tensor(negk.astype(ml_dtypes.bfloat16), name="c_negk").ap()
    neg256_d = nc.inline_tensor(np.full((128, 1), -256.0, np.float32), name="c_n256").ap()
    a2 = np.stack([np.arange(128) - 255.0, np.arange(128) - 127.0],
                  axis=1).astype(np.float32)
    a2_d = nc.inline_tensor(a2, name="c_a2").ap()

    es = contextlib.ExitStack()
    sb = lambda name, shape, dt=F32: es.enter_context(nc.sbuf_tensor(name, shape, dt))
    psum = lambda name: es.enter_context(nc.psum_tensor(name, [128, 512], F32))

    ident = sb("ident", [128, 128])
    ones_col = sb("ones_col", [128, 1])
    ones_row = sb("ones_row", [1, 128])
    miota = sb("miota", [128, 512], BF16)
    negm1_r = sb("negm1_r", [1, 256], BF16)
    negk_r = sb("negk_r", [1, 256], BF16)
    neg256 = sb("neg256", [128, 1])
    epsln = sb("epsln", [1, 1])
    epsrms = sb("epsrms", [1, 1])
    gs_t = sb("gs_t", [128, 44])
    us_t = sb("us_t", [128, 44])
    ds_t = sb("ds_t", [128, 16])
    sl_t = sb("sl_t", [128, 16])
    bd_t = sb("bd_t", [128, 16])
    rowscr = sb("rowscr", [1, 512])
    qdbc = sb("qdbc", [128, 4096])
    kd_cols = sb("kd_cols", [128, 32])
    a2_t = sb("a2_t", [128, 2])
    diagd = sb("diagd", [128, 512])
    xarena = sb("xarena", [128, 4096])
    x1arena = sb("x1arena", [128, 4096])
    x1lo = sb("x1lo", [128, 4096], BF16)
    big = sb("big", [128, 16384])
    oarena = sb("oarena", [128, 4096])
    kvA = sb("kvA", [128, 2048])
    kvB = sb("kvB", [128, 2048])
    vts = sb("vts", [128, 256])
    kts = sb("kts", [128, 256])
    qdec = sb("qdec", [128, 256])
    kqd = sb("kqd", [128, 512])
    sq0 = sb("sq0", [128, 256])
    sq1 = sb("sq1", [128, 256])
    stage = sq0
    stageg = sq1
    rb1 = sb("rb1", [128, 512])
    m_row = sb("m_row", [1, 256])
    m2_row = sb("m2_row", [1, 256])
    var_row = sb("var_row", [1, 256])
    sd_row = sb("sd_row", [1, 256])
    R_row = sb("R_row", [1, 256])
    MR_row = sb("MR_row", [1, 256])
    warena = sb("warena", [128, 12288], BF16)

    r3b = rb1[:, 0:256]

    pA = psum("pA")
    pB = psum("pB")
    pq = psum("pq")
    ptr = psum("ptr")
    po = psum("po")
    pkv = psum("pkv")
    pstx = psum("pstx")
    pstq = psum("pstq")

    L = Ledger(nc, es, ["cl"] + ["wl%d" % i for i in range(6)]
                       + ["wr%d" % i for i in range(6)]
                       + ["xl", "mm", "ve", "se", "stA", "stB", "gp",
                          "ppsA", "ppsB", "pplA", "pplB"])

    TEN, VEC, SCA, SYN, GPS = nc.tensor, nc.vector, nc.scalar, nc.sync, nc.gpsimd

    def hi_view(tensor, col0, ncols):
        return tensor[:].bitcast(BF16)[:, 2 * col0 + 1:2 * (col0 + ncols):2]

    def xc(c):
        return xarena[:, c * 256:(c + 1) * 256]

    def x1c(c):
        return x1arena[:, c * 256:(c + 1) * 256]

    def x1hi(c):
        return hi_view(x1arena, c * 256, 256)

    def x1loc(c):
        return x1lo[:, c * 256:(c + 1) * 256]

    def oc(c):
        return oarena[:, c * 256:(c + 1) * 256]

    def qh(h):
        return big[:, h * 256:(h + 1) * 256]

    def kh(h, bset=0):
        base = 4096 if bset == 0 else 0
        return big[:, base + h * 256:base + (h + 1) * 256]

    def vh(h, bset=0):
        base = 8192 if bset == 0 else 12288
        return big[:, base + h * 256:base + (h + 1) * 256]

    def uh(h):
        return big[:, 12288 + h * 256:12288 + (h + 1) * 256]

    def x3c(c):
        return big[:, c * 256:(c + 1) * 256]

    def x3hi(c):
        return hi_view(big, c * 256, 256)

    def sgc(par):
        return big[:, 4096 + par * 256:4096 + (par + 1) * 256]

    def hdnhi(c):
        return big[:].bitcast(BF16)[:, 9216 + c * 256:9216 + (c + 1) * 256]

    def hdnlo(c):
        return big[:].bitcast(BF16)[:, 20480 + c * 256:20480 + (c + 1) * 256]

    state = {"wctr": 0, "wgoal": {}, "ready": {}, "bankw": {}, "bankr": {}, "blk": 0}
    R = state["ready"]

    def load_w(dram_ap, ncols, family, f32=False):
        i = state["wctr"]
        state["wctr"] += 1
        slot = i % 6
        if f32:
            wf = warena[:].bitcast(F32)
            dst = wf[:, slot * 1024:slot * 1024 + ncols]
        else:
            dst = warena[:, slot * 2048:slot * 2048 + ncols]
        if family is not None:
            if isinstance(family, list):
                for fk in family:
                    L.wait(SYN, "SYN", R.get(fk, R.get(("wfam", "ffn"), None)))
            else:
                L.wait(SYN, "SYN", R.get(("wfam", family), None))
        if i >= 6:
            L.wait(SYN, "SYN", ("wr%d" % slot, i // 6))
        state["wgoal"][i] = L.inc(SYN.dma_start(out=dst, in_=dram_ap),
                                  "wl%d" % slot, 16)
        return i, dst, slot * 2048

    def w_release_name(i):
        return "wr%d" % (i % 6)

    def proj_matmuls_hilo(segments, xhi_fn, xlo_fn, bank, extra_waits=(),
                          w_split=True, x_marks=None):
        for w in extra_waits:
            L.wait(TEN, "TEN", w)
        mark = None
        first = True
        last_seg = len(segments) - 1
        for si, (cid, wsl, ks, strided_hi) in enumerate(segments):
            L.wait(TEN, "TEN", state["wgoal"][cid])
            if first:
                L.wait(TEN, "TEN", state["bankr"].get(bank.name, None))
            for ki, k in enumerate(ks):
                final = (si == last_seg and ki == len(ks) - 1)
                if x_marks is not None:
                    L.wait(TEN, "TEN", x_marks[k])
                if w_split:
                    whi = wsl[:, ki * 128:(ki + 1) * 128]
                    wlo = wsl[:, 1024 + ki * 128:1024 + (ki + 1) * 128]
                    TEN.matmul(bank[:, 0:256], whi, xhi_fn(k), start=first,
                               stop=False)
                    TEN.matmul(bank[:, 0:256], whi, xlo_fn(k), start=False,
                               stop=False)
                    instr = TEN.matmul(bank[:, 0:256], wlo, xhi_fn(k), start=False,
                                       stop=final)
                else:
                    if strided_hi:
                        whi = wsl[:, 2 * ki * 128 + 1:2 * (ki + 1) * 128:2]
                    else:
                        whi = wsl[:, ki * 128:(ki + 1) * 128]
                    TEN.matmul(bank[:, 0:256], whi, xhi_fn(k), start=first,
                               stop=False)
                    instr = TEN.matmul(bank[:, 0:256], whi, xlo_fn(k), start=False,
                                       stop=final)
                first = False
                if ki == len(ks) - 1:
                    L.inc(instr, w_release_name(cid), 1)
                    if final:
                        mark = L.cur(w_release_name(cid))
        state["bankw"][bank.name] = mark
        return mark

    def bank_read_done(bank, mark):
        state["bankr"][bank.name] = [mark]

    def bank_read_add(bank, mark):
        state["bankr"].setdefault(bank.name, []).append(mark)

    with es:
        es.enter_context(nc.Block())

        for dst, src in [(ident, ident_d), (ones_col, ones_col_d),
                         (ones_row, ones_row_d), (miota, miota_d),
                         (negm1_r, negm1_d), (negk_r, negk_d), (neg256, neg256_d),
                         (a2_t, a2_d),
                         (gs_t, gs_d), (us_t, us_d), (ds_t, ds_d), (sl_t, sl_d)]:
            L.inc(GPS.dma_start(out=dst[:], in_=src[:, :]), "cl", 16)
        setup_xl = L.cur("cl")

        VEC.memset(epsln[:], EPS_LN)
        VEC.memset(epsrms[:], EPS_RMS)
        L.inc(VEC.memset(kvA[:], 0.0), "ve", 1)
        R["kv0"] = L.cur("ve")
        L.wait(VEC, "VEC", setup_xl)

        def pp_in(s):
            return oarena[:, s * 2048:s * 2048 + 1024]

        def pp_out(s):
            return oarena[:].bitcast(BF16)[:, s * 4096 + 2048:(s + 1) * 4096]

        PPC = 1024
        pp_jobs = []
        kv_order = []
        for h in range(16):
            kv_order += [16 + h, 32 + h]
        order = kv_order + list(range(16)) + list(range(48, 64))
        for fo in order:
            for pc in range(2):
                pp_jobs.append((wq_d[fo][:, pc * PPC:(pc + 1) * PPC],
                                wql_d[fo][:, pc * 2048:(pc + 1) * 2048], PPC, True,
                                ("wpp", "q", fo, pc)))
        n_kv_jobs = 64
        for fo in range(16):
            for pc in range(2):
                pp_jobs.append((wo_d[fo][:, pc * PPC:(pc + 1) * PPC],
                                wol_d[fo][:, pc * 2048:(pc + 1) * 2048], PPC, True,
                                ("wpp", "o", fo, pc)))

        for fo in range(44):
            for pc in range(2):
                pp_jobs.append((wg_d[fo][:, pc * PPC:(pc + 1) * PPC],
                                wgb_d[fo][:, pc * PPC:(pc + 1) * PPC], PPC, False,
                                ("wpp", "g", fo, pc)))
        for fo in range(44):
            for pc in range(2):
                pp_jobs.append((wu_d[fo][:, pc * PPC:(pc + 1) * PPC],
                                wub_d[fo][:, pc * PPC:(pc + 1) * PPC], PPC, False,
                                ("wpp", "u", fo, pc)))
        for ho in range(16):
            for pc in range(8):
                pp_jobs.append((wd_d[ho][:, pc * 704:(pc + 1) * 704],
                                wdb_d[ho][:, pc * 704:(pc + 1) * 704], 704, False,
                                ("wpp", "d", ho, pc)))

        pp_done = {}
        pp_state = {"next": 0}

        def pp_emit(count):
            end = min(pp_state["next"] + count, len(pp_jobs))
            for i in range(pp_state["next"], end):
                srcap, dstp, ncols, split, ckey = pp_jobs[i]
                s = i % 2
                L.wait(SYN, "SYN", pp_done.get(i - 2, None))
                ld = L.inc(SYN.dma_start(out=pp_in(s)[:, 0:ncols], in_=srcap),
                           "pplA" if s == 0 else "pplB", 16)
                hi_ap = pp_out(s)[:, 0:ncols]
                L.wait(SCA, "SCA", ld, pp_done.get(i - 2, None))
                h1 = L.inc(SCA.activation(hi_ap, pp_in(s)[:, 0:ncols], AF.Copy),
                           "se", 1)
                if split:
                    lo_ap = pp_out(s)[:, 1024:1024 + ncols]
                    L.wait(VEC, "VEC", h1, pp_done.get(i - 2, None))
                    h2 = L.inc(VEC.tensor_tensor(lo_ap, pp_in(s)[:, 0:ncols],
                                                 hi_ap, ALU.subtract), "ve", 1)
                    L.wait(GPS, "GPS", h2)
                    stm = L.inc(GPS.dma_start(out=dstp, in_=pp_out(s)[:, 0:2048]),
                                "ppsA" if s == 0 else "ppsB", 16)
                else:
                    L.wait(GPS, "GPS", h1)
                    stm = L.inc(GPS.dma_start(out=dstp, in_=pp_out(s)[:, 0:ncols]),
                                "ppsA" if s == 0 else "ppsB", 16)
                pp_done[i] = stm
                if ckey is not None:
                    R[ckey] = stm
                if i == n_kv_jobs - 1:
                    R[("wfam", "kv")] = [L.cur("ppsA"), L.cur("ppsB")]
                if i == 128 - 1:
                    R[("wfam", "qu")] = [L.cur("ppsA"), L.cur("ppsB")]
                if i == 160 - 1:
                    R[("wfam", "wo")] = [L.cur("ppsA"), L.cur("ppsB")]
                if i == len(pp_jobs) - 1:
                    R[("wfam", "ffn")] = [L.cur("ppsA"), L.cur("ppsB")]
                    R["pp_oarena_free"] = R[("wfam", "ffn")]
            pp_state["next"] = end


        L.wait(SCA, "SCA", setup_xl)
        emit_ln0 = True
        for h in range(HEADS):
            instr = SCA.activation(bd_t[:, h:h + 1], neg256[:, 0:1],
                                   AF.Exp, scale=sl_t[:, h:h + 1])
        dr_mark = L.inc(instr, "se", 1)
        for h in range(HEADS):
            instr = SCA.activation(kd_cols[:, 2 * h:2 * h + 2], a2_t[:, 0:2],
                                   AF.Exp, scale=sl_t[:, h:h + 1])
        R["kdcols"] = L.inc(instr, "se", 1)
        for h in range(HEADS):
            L.wait(SCA, "SCA", L.cur("mm"))
            se_h = L.inc(SCA.activation(rowscr[0:1, 0:256], negm1_r[0:1, :],
                                        AF.Exp, scale=sl_t[0:1, h:h + 1]), "se", 1)
            L.wait(TEN, "TEN", se_h, state["bankr"].get(pq.name, None))
            mmk = L.inc(TEN.matmul(pq[:, 0:256], ones_row[:], rowscr[0:1, 0:256],
                                   start=True, stop=True), "mm", 1)
            L.wait(VEC, "VEC", mmk)
            ev = L.inc(VEC.tensor_copy(qdbc[:, h * 256:(h + 1) * 256], pq[:, 0:256]),
                       "ve", 1)
            bank_read_done(pq, ev)
        R["qdkd"] = L.cur("ve")
        R["bd"] = dr_mark
        for h in range(HEADS):
            R[("kv_ok", 0, h)] = R["kv0"]

        def ln_stats_chunk(x_ap, c, waits=()):
            L.wait(SCA, "SCA", R.get(("sq_busy", c % 2), None), *waits)
            se = L.inc(SCA.square(sq0[:] if c % 2 == 0 else sq1[:], x_ap), "se", 1)
            L.wait(TEN, "TEN", se)
            if c == 0:
                L.wait(TEN, "TEN", state["bankr"].get(pstx.name, None),
                       state["bankr"].get(pstq.name, None))
            TEN.matmul(pstx[0:1, 0:256], ones_col[:], x_ap,
                       start=(c == 0), stop=(c == NCH - 1))
            instr = TEN.matmul(pstq[0:1, 0:256], ones_col[:],
                               sq0[:] if c % 2 == 0 else sq1[:],
                               start=(c == 0), stop=(c == NCH - 1))
            R[("sq_busy", c % 2)] = L.inc(instr, "mm", 1)

        def layernorm(x_chunk_fn, dst_fn, dstlo_fn, rb, src_waits, dst_waits=None,
                      split_apply=True):
            for c in range(NCH):
                if c == 0:
                    L.wait(SCA, "SCA", src_waits)
                    L.wait(TEN, "TEN", src_waits)
                ln_stats_chunk(x_chunk_fn(c), c)
            return ln_finish(x_chunk_fn, dst_fn, dstlo_fn, rb, dst_waits,
                             split_apply)

        def ln_finish(x_chunk_fn, dst_fn, dstlo_fn, rb, dst_waits=None,
                      split_apply=True):
            mm_stats = L.cur("mm")
            L.wait(SCA, "SCA", mm_stats)
            cp = L.inc(SCA.activation(m_row[0:1, :], pstx[0:1, 0:256], AF.Copy,
                                      scale=1.0 / DM), "se", 1)
            L.wait(SCA, "SCA", cp)
            se1 = L.inc(SCA.square(m2_row[0:1, :], m_row[0:1, :]), "se", 1)
            bank_read_done(pstx, se1)
            L.wait(VEC, "VEC", se1)
            ve1 = L.inc(VEC.scalar_tensor_tensor(var_row[0:1, :], pstq[0:1, 0:256],
                                                 1.0 / DM, m2_row[0:1, :],
                                                 ALU.mult, ALU.subtract), "ve", 1)
            bank_read_done(pstq, ve1)
            L.wait(SCA, "SCA", ve1)
            se2 = L.inc(SCA.activation(sd_row[0:1, :], var_row[0:1, :], AF.Sqrt,
                                       bias=epsln[0:1, 0:1]), "se", 1)
            L.wait(VEC, "VEC", se2)
            rv = L.inc(VEC.reciprocal(R_row[0:1, :], sd_row[0:1, :]), "ve", 1)
            L.wait(VEC, "VEC", rv)
            ve2 = L.inc(VEC.tensor_mul(MR_row[0:1, :], m_row[0:1, :], R_row[0:1, :]),
                        "ve", 1)
            L.wait(TEN, "TEN", ve2, state["bankr"].get(pq.name, None))
            TEN.matmul(pq[:, 0:256], ones_row[:], R_row[0:1, :], start=True, stop=True)
            instr = TEN.matmul(pq[:, 256:512], ones_row[:], MR_row[0:1, :],
                               start=True, stop=True)
            mmb = L.inc(instr, "mm", 1)
            L.wait(VEC, "VEC", mmb)
            ve3 = L.inc(VEC.tensor_copy(rb[:, 0:512], pq[:, 0:512]), "ve", 1)
            bank_read_done(pq, ve3)
            L.wait(VEC, "VEC", ve3)
            L.wait(GPS, "GPS", ve3)
            chunk_marks = []
            for c in range(NCH):
                use_gps = split_apply and (c % 3 == 2)
                eng, en, sem, stg2 = ((GPS, "GPS", "gp", stageg) if use_gps
                                      else (VEC, "VEC", "ve", stage))
                if dst_waits is not None:
                    L.wait(eng, en, dst_waits(c))
                m1 = L.inc(eng.tensor_tensor(stg2[:], x_chunk_fn(c), rb[:, 0:256],
                                             ALU.mult), sem, 1)
                L.wait(eng, en, m1)
                m2 = L.inc(eng.tensor_tensor(dst_fn(c), stg2[:], rb[:, 256:512],
                                             ALU.subtract), sem, 1)
                L.wait(eng, en, m2)
                dst_ap = dst_fn(c)
                m3 = L.inc(eng.tensor_tensor(dstlo_fn(c), dst_ap,
                                             dst_ap.bitcast(BF16)[:, 1::2],
                                             ALU.subtract), sem, 1)
                L.wait(eng, en, m3)
                chunk_marks.append(m3)
            return chunk_marks

        def attn_A(h, dup, bset=0):
            if not dup:
                L.wait(SCA, "SCA", R.get(("diagd_free",), None))
                sed = L.inc(SCA.activation(diagd[:], miota[:], AF.Exp,
                                           scale=sl_t[:, h:h + 1]), "se", 1)
                L.wait(TEN, "TEN", state["bankr"].get(pq.name, None),
                       R[("q", h)], R[("k", h, 0)])
                TEN.matmul(pq[:, 0:256], kh(h)[:, 0:128], qh(h), start=True, stop=True)
                mmq = L.inc(TEN.matmul(pq[:, 256:512], kh(h)[:, 128:256], qh(h),
                                       start=True, stop=True), "mm", 1)
                state["bankw"][pq.name] = mmq
                R[("mmq", h)] = mmq
            if not dup:
                L.wait(VEC, "VEC", R[("mmq", h)], sed, R.get(("kqd_free",), None))
                VEC.tensor_tensor(kqd[:, 0:256], pq[:, 0:256], diagd[:, 0:256],
                                  ALU.mult)
                vek = L.inc(VEC.tensor_tensor(kqd[:, 256:512], pq[:, 256:512],
                                              diagd[:, 256:512], ALU.mult), "ve", 1)
                bank_read_done(pq, vek)
                R[("diagd_free",)] = vek
                R[("vek", h)] = vek
                L.wait(VEC, "VEC", R[("q", h)], R["qdkd"],
                       R.get(("qdec_free",), None))
                veq = L.inc(VEC.tensor_tensor(qdec[:], qh(h),
                                              qdbc[:, h * 256:(h + 1) * 256],
                                              ALU.mult), "ve", 1)
                R[("veq", h)] = veq
                R[("q_used", h)] = [R[("mmq", h)], veq]
            L.wait(TEN, "TEN", state["bankr"].get(ptr.name, None),
                   R[("v", h, bset)], R[("k", h, bset)])
            TEN.transpose(ptr[:, 0:128], vh(h, bset)[:, 0:128], ident[:])
            TEN.transpose(ptr[:, 128:256], vh(h, bset)[:, 128:256], ident[:])
            TEN.transpose(ptr[:, 256:384], kh(h, bset)[:, 0:128], ident[:])
            mmt = L.inc(TEN.transpose(ptr[:, 384:512], kh(h, bset)[:, 128:256],
                                      ident[:]), "mm", 1)
            R[("v_used", h, bset)] = mmt
            L.wait(VEC, "VEC", mmt, R["kdcols"], R.get(("vts_free",), None),
                   R.get(("outbuf", 0), None), R.get(("outbuf", 1), None))
            VEC.tensor_copy(vts[:], ptr[:, 0:256])
            VEC.tensor_scalar(kts[:, 0:128], ptr[:, 256:384],
                              kd_cols[:, 2 * h:2 * h + 1], None, ALU.mult)
            vet = L.inc(VEC.tensor_scalar(kts[:, 128:256], ptr[:, 384:512],
                                          kd_cols[:, 2 * h + 1:2 * h + 2], None,
                                          ALU.mult), "ve", 1)
            bank_read_done(ptr, vet)
            R[("vet", h)] = vet
            R[("k_used", h, bset)] = [mmt] + ([R[("mmq", h)]] if not dup else [])

        def attn_B(h, dup, first_main, blk=None):
            parity = (state["blk"] if blk is None else blk) % 2
            kv_cur = kvA if parity == 0 else kvB
            kv_nxt = kvB if parity == 0 else kvA
            kvs_cur = kv_cur[:, h * 128:(h + 1) * 128]
            kvs_nxt = kv_nxt[:, h * 128:(h + 1) * 128]

            if not dup:
                L.wait(TEN, "TEN", state["bankr"].get(po.name, None),
                       R[("vet", h)], R[("vek", h)], R[("veq", h)],
                       R[("kv_ok", parity, h)])
                TEN.matmul(po[:, 0:256], vts[:, 0:128], kqd[:, 0:256],
                           start=True, stop=False)
                TEN.matmul(po[:, 0:256], vts[:, 128:256], kqd[:, 256:512],
                           start=False, stop=False)
                mmo = L.inc(TEN.matmul(po[:, 0:256], kvs_cur, qdec[:],
                                       start=False, stop=True), "mm", 1)
                state["bankw"][po.name] = mmo
                R[("kv_read", parity, h)] = mmo
                R[("kqd_free",)] = mmo
                R[("qdec_free",)] = mmo
            else:
                L.wait(TEN, "TEN", R[("vet", h)])
                mmo = None
            L.wait(TEN, "TEN", state["bankr"].get(pkv.name, None))
            TEN.matmul(pkv[:, 0:128], kts[:, 0:128], vts[:, 0:128],
                       start=True, stop=False)
            mmk2 = L.inc(TEN.matmul(pkv[:, 0:128], kts[:, 128:256], vts[:, 128:256],
                                    start=False, stop=True), "mm", 1)
            R[("vts_free",)] = mmk2
            if not dup and h >= 1:
                sqb_prev = sq0 if (h - 1) % 2 == 0 else sq1
                L.wait(TEN, "TEN", R[("seo", h - 1)])
                if h - 1 == 0:
                    L.wait(TEN, "TEN", state["bankr"].get(pstx.name, None))
                instr = TEN.matmul(pstx[0:1, 0:256], ones_col[:], sqb_prev[:],
                                   start=(h - 1 == 0), stop=False)
                R[("o2_free", (h - 1) % 2)] = L.inc(instr, "mm", 1)
            L.wait(VEC, "VEC", mmk2, R["bd"], R[("kv_ok", parity, h)],
                   R.get(("kv_read", parity, h), None),
                   R.get(("kv_read", 1 - parity, h), None))
            vkv = L.inc(VEC.scalar_tensor_tensor(kvs_nxt, kvs_cur, bd_t[:, h:h + 1],
                                                 pkv[:, 0:128], ALU.mult, ALU.add),
                        "ve", 1)
            bank_read_done(pkv, vkv)
            R[("kv_ok", 1 - parity, h)] = vkv

            if not dup:
                L.wait(VEC, "VEC", mmo)
                if first_main and h == 0:
                    L.wait(VEC, "VEC", R["pp_oarena_free"])
                veo = L.inc(VEC.tensor_copy(oc(h), po[:, 0:256]), "ve", 1)
                sqb = sq0 if h % 2 == 0 else sq1
                L.wait(SCA, "SCA", mmo, R.get(("o2_free", h % 2), None))
                seo = L.inc(SCA.square(sqb[:], po[:, 0:256]), "se", 1)
                R[("seo", h)] = seo
                bank_read_done(po, veo)
                bank_read_add(po, seo)

        def emit_x_load(t, dup):
            x_dram = xd_d if dup else xm_d
            L.wait(GPS, "GPS", R.get("x_free", None))
            xin = xarena[:].rearrange("p (c m) -> p c m", c=NCH)
            return L.inc(GPS.dma_start(out=xin,
                                       in_=x_dram[t].rearrange("c p m -> p c m")),
                         "xl", 16)

        def emit_tile(t, dup, first_main=False):

            pre = R.pop(("x1_ready", dup, t), None)
            if pre is not None:
                x1_ready = pre
            else:
                key = ("tile_xl", dup, t)
                tile_xl = R.pop(key, None) or emit_x_load(t, dup)
                x1_ready = layernorm(xc, x1c, x1loc, rb1, tile_xl,
                                     dst_waits=lambda c: R.get("x1_free", None))
            R["x1"] = x1_ready

            banks = [pA, pB]
            pidx = [0]

            def proj_chunk(fo, bset=0):
                segs = []
                for half in range(2):
                    cid, wsl, _ = load_w(wql_d[fo][:, half * 2048:(half + 1) * 2048],
                                         2048, [("wpp", "q", fo, half)])
                    segs.append((cid, wsl, list(range(half * 8, half * 8 + 8)), False))
                bank = banks[pidx[0] % 2]
                pidx[0] += 1
                mark = proj_matmuls_hilo(segs, x1hi, x1loc, bank,
                                         x_marks=x1_ready)
                role, hh = divmod(fo, 16)
                if role == 0:
                    L.wait(SCA, "SCA", mark, R.get(("q_used", hh), None))
                    ev = L.inc(SCA.activation(qh(hh), bank[:, 0:256], AF.Silu),
                               "se", 1)
                    R[("q", hh)] = ev
                elif role == 1:
                    L.wait(SCA, "SCA", mark, R.get(("k_used", hh, bset), None))
                    ev = L.inc(SCA.activation(kh(hh, bset), bank[:, 0:256],
                                              AF.Silu), "se", 1)
                    R[("k", hh, bset)] = ev
                elif role == 2:
                    L.wait(SCA, "SCA", mark, R.get(("v_used", hh, bset), None))
                    ev = L.inc(SCA.copy(vh(hh, bset), bank[:, 0:256]), "se", 1)
                    R[("v", hh, bset)] = ev
                else:
                    L.wait(SCA, "SCA", mark, R.get(("u_used", hh), None))
                    ev = L.inc(SCA.copy(uh(hh), bank[:, 0:256]), "se", 1)
                    R[("u", hh)] = ev
                bank_read_done(bank, ev)
                R["last_proj_mark"] = mark

            if dup:
                bset = t % 2
                prev = t - 1
                for h in range(HEADS):
                    for role in (1, 2):
                        proj_chunk(role * 16 + h, bset=bset)
                        pp_emit(2)
                    if prev >= 0:
                        if h >= 1:
                            attn_B(h - 1, True, False, blk=prev)
                        attn_A(h, True, bset=prev % 2)
                if prev >= 0:
                    attn_B(HEADS - 1, True, False, blk=prev)
                R["x1_free"] = R["last_proj_mark"]
                R["x_free"] = x1_ready
                if t + 1 < n_dup:
                    xl_next = emit_x_load(t + 1, True)
                    R[("x1_ready", True, t + 1)] = layernorm(
                        xc, x1c, x1loc, rb1, xl_next,
                        dst_waits=lambda c: R.get("x1_free", None))
                else:
                    for h in range(HEADS):
                        if h >= 1:
                            attn_B(h - 1, True, False, blk=t)
                        attn_A(h, True, bset=t % 2)
                    attn_B(HEADS - 1, True, False, blk=t)
                    if n_main > 0:
                        xl_next = emit_x_load(0, False)
                        R[("x1_ready", False, 0)] = layernorm(
                            xc, x1c, x1loc, rb1, xl_next,
                            dst_waits=lambda c: R.get("x1_free", None))
                state["blk"] += 1
                return

            roles = (0, 1, 2, 3)
            for h in range(HEADS):
                for role in roles:
                    proj_chunk(role * 16 + h)
                if h >= 1:
                    attn_B(h - 1, dup, first_main)
                attn_A(h, dup)
            attn_B(HEADS - 1, dup, first_main)
            if not dup:
                hh = HEADS - 1
                sqb_prev = sq0 if hh % 2 == 0 else sq1
                L.wait(TEN, "TEN", R[("seo", hh)])
                instr = TEN.matmul(pstx[0:1, 0:256], ones_col[:], sqb_prev[:],
                                   start=False, stop=True)
                R[("o2_free", hh % 2)] = L.inc(instr, "mm", 1)
            state["blk"] += 1

            mm_rms = L.cur("mm")
            L.wait(SCA, "SCA", mm_rms)
            se1 = L.inc(SCA.activation(m_row[0:1, :], pstx[0:1, 0:256], AF.Copy,
                                       scale=1.0 / DM), "se", 1)
            bank_read_done(pstx, se1)
            L.wait(SCA, "SCA", se1)
            se2 = L.inc(SCA.activation(sd_row[0:1, :], m_row[0:1, :], AF.Sqrt,
                                       bias=epsrms[0:1, 0:1]), "se", 1)
            L.wait(VEC, "VEC", se2)
            ve1 = L.inc(VEC.reciprocal(R_row[0:1, :], sd_row[0:1, :]), "ve", 1)
            L.wait(TEN, "TEN", ve1, state["bankr"].get(pq.name, None))
            mmb = L.inc(TEN.matmul(pq[:, 0:256], ones_row[:], R_row[0:1, :],
                                   start=True, stop=True), "mm", 1)
            L.wait(VEC, "VEC", mmb)
            ve2 = L.inc(VEC.tensor_copy(r3b, pq[:, 0:256]), "ve", 1)
            bank_read_done(pq, ve2)

            L.wait(VEC, "VEC", R[("u", NCH - 1)])
            L.wait(VEC, "VEC", ve2)
            L.wait(GPS, "GPS", R[("u", NCH - 1)], ve2)
            x2in_marks = []
            for c in range(NCH):
                use_gps = (c % 3 == 2)
                eng, en, sem, stg2 = ((GPS, "GPS", "gp", stageg) if use_gps
                                      else (VEC, "VEC", "ve", stage))
                L.wait(eng, en, R[("u", c)])
                m1 = L.inc(eng.tensor_tensor(stg2[:], oc(c), r3b, ALU.mult),
                           sem, 1)
                L.wait(eng, en, m1)
                m2 = L.inc(eng.tensor_tensor(x1c(c), stg2[:], uh(c), ALU.mult),
                           sem, 1)
                L.wait(eng, en, m2)
                m3 = L.inc(eng.tensor_tensor(x1loc(c), x1c(c), x1hi(c),
                                             ALU.subtract), sem, 1)
                L.wait(eng, en, m3)
                x2in_marks.append(m3)
            x2in_ready = [x2in_marks[NCH - 1], x2in_marks[NCH - 2]]
            for c in range(NCH):
                R[("u_used", c)] = x2in_ready

            x_free_marks = []
            for fo in range(NCH):
                segs = []
                for half in range(2):
                    cid, wsl, _ = load_w(wol_d[fo][:, half * 2048:(half + 1) * 2048],
                                         2048, [("wpp", "o", fo, half)])
                    segs.append((cid, wsl, list(range(half * 8, half * 8 + 8)), False))
                bank = banks[fo % 2]
                mark = proj_matmuls_hilo(segs, x1hi, x1loc, bank,
                                         x_marks=x2in_marks)
                L.wait(VEC, "VEC", mark)
                ev = L.inc(VEC.tensor_tensor(oc(fo), bank[:, 0:256], xc(fo),
                                             ALU.add), "ve", 1)
                bank_read_done(bank, ev)
                x_free_marks.append(ev)
            R["x_free"] = x_free_marks[-1]
            R["x2"] = x_free_marks
            R["x1_free"] = x_free_marks[-1]

            x3_ready = layernorm(oc, x3c, x1loc, rb1, R["x2"],
                                 dst_waits=lambda c: [R.get(("q_used", c), None),
                                                      R["x1_free"]])

            if t + 1 < n_main:
                R[("tile_xl", False, t + 1)] = emit_x_load(t + 1, False)

            for fo in range(ICH):
                par = fo % 2
                cidg, wslg, _ = load_w(wgb_d[fo], 2048, [("wpp", "g", fo, 0), ("wpp", "g", fo, 1)])
                markg = proj_matmuls_hilo([(cidg, wslg, list(range(NCH)), False)],
                                          x3hi, x1loc, pA,
                                          w_split=False, x_marks=x3_ready)
                L.wait(SCA, "SCA", markg, R.get(("sg_used", par), None))
                seg = L.inc(SCA.activation(sgc(par), pA[:, 0:256], AF.Silu,
                                           scale=gs_t[:, fo:fo + 1]), "se", 1)
                bank_read_done(pA, seg)
                cidu, wslu, _ = load_w(wub_d[fo], 2048, [("wpp", "u", fo, 0), ("wpp", "u", fo, 1)])
                marku = proj_matmuls_hilo([(cidu, wslu, list(range(NCH)), False)],
                                          x3hi, x1loc, pB,
                                          w_split=False, x_marks=x3_ready)
                L.wait(VEC, "VEC", marku, seg)
                n1 = L.inc(VEC.scalar_tensor_tensor(stage[:], pB[:, 0:256],
                                                    us_t[:, fo:fo + 1], sgc(par),
                                                    ALU.mult, ALU.mult), "ve", 1)
                bank_read_done(pB, n1)
                L.wait(VEC, "VEC", n1)
                n2 = L.inc(VEC.tensor_copy(hdnhi(fo), stage[:]), "ve", 1)
                L.wait(VEC, "VEC", n2)
                n3 = L.inc(VEC.tensor_tensor(hdnlo(fo), stage[:], hdnhi(fo),
                                             ALU.subtract), "ve", 1)
                L.wait(VEC, "VEC", n3)
                R[("sg_used", par)] = n1
            hdn_ready = L.cur("ve")

            if t + 1 < n_main:
                xl_next = R.pop(("tile_xl", False, t + 1))
                R[("x1_ready", False, t + 1)] = layernorm(
                    xc, x1c, x1loc, rb1, [xl_next, hdn_ready],
                    dst_waits=lambda c: R.get("x1_free", None))

            for ho in range(NCH):
                bank = banks[ho % 2]
                for quarter in range(4):
                    cid, wsl, _ = load_w(
                        wdb_d[ho][:, quarter * 1408:(quarter + 1) * 1408],
                        1408, [("wpp", "d", ho, 2 * quarter),
                               ("wpp", "d", ho, 2 * quarter + 1)])
                    L.wait(TEN, "TEN", state["wgoal"][cid])
                    if quarter == 0:
                        L.wait(TEN, "TEN", state["bankr"].get(bank.name, None),
                               hdn_ready)
                    for k in range(11):
                        kk = quarter * 11 + k
                        TEN.matmul(bank[:, 0:256], wsl[:, k * 128:(k + 1) * 128],
                                   hdnhi(kk), start=(kk == 0), stop=False)
                        instr = TEN.matmul(bank[:, 0:256],
                                           wsl[:, k * 128:(k + 1) * 128],
                                           hdnlo(kk), start=False,
                                           stop=(kk == ICH - 1 and k == 10))
                    mark = L.inc(instr, w_release_name(cid), 1)
                state["bankw"][bank.name] = mark
                par = ho % 2
                obuf = vts if par == 0 else kts
                L.wait(VEC, "VEC", mark, R.get(("outbuf", par), None))
                ev = L.inc(VEC.scalar_tensor_tensor(obuf[:],
                                                    bank[:, 0:256], ds_t[:, ho:ho + 1],
                                                    oc(ho), ALU.mult, ALU.add),
                           "ve", 1)
                bank_read_done(bank, ev)
                L.wait(GPS, "GPS", ev)
                stm = L.inc(GPS.dma_start(out=out_d[t][ho], in_=obuf[:]),
                            "stA" if par == 0 else "stB", 16)
                R[("outbuf", par)] = stm

        if n_dup > 0:
            xl0 = emit_x_load(0, True)
            R[("x1_ready", True, 0)] = layernorm(
                xc, x1c, x1loc, rb1, xl0, dst_waits=None)
        pp_emit(n_kv_jobs)

        for t in range(n_dup):
            emit_tile(t, dup=True)
        pp_emit(len(pp_jobs))
        for t in range(n_main):
            emit_tile(t, dup=False, first_main=(t == 0))

        GPS.wait_ge(L.h["stA"], L.c["stA"])
        GPS.wait_ge(L.h["stB"], L.c["stB"])

    return nc


def _prep_inputs(L0_attn_x, L0_attn_slope_rate, L0_attn_W_qkvu, L0_attn_W_out,
                 L0_ffn_gate_w, L0_ffn_up_w, L0_ffn_down_w, L0_ffn_gate_s,
                 L0_ffn_up_s, L0_ffn_down_s):
    f32 = np.float32
    x = np.ascontiguousarray(L0_attn_x, dtype=f32)
    wq = np.ascontiguousarray(
        np.asarray(L0_attn_W_qkvu, f32).reshape(64, 128, 16, 128)
        .transpose(0, 3, 2, 1).reshape(64, 128, 2048))
    wo = np.ascontiguousarray(
        np.asarray(L0_attn_W_out, f32).reshape(16, 128, 16, 128)
        .transpose(0, 3, 2, 1).reshape(16, 128, 2048))
    wg = np.ascontiguousarray(
        np.asarray(L0_ffn_gate_w, f32).reshape(44, 128, 16, 128)
        .transpose(0, 3, 2, 1).reshape(44, 128, 2048))
    wu = np.ascontiguousarray(
        np.asarray(L0_ffn_up_w, f32).reshape(44, 128, 16, 128)
        .transpose(0, 3, 2, 1).reshape(44, 128, 2048))
    wd = np.ascontiguousarray(
        np.asarray(L0_ffn_down_w, f32).reshape(16, 128, 44, 128)
        .transpose(0, 3, 2, 1).reshape(16, 128, 5632))
    gs = np.ascontiguousarray(np.asarray(L0_ffn_gate_s, f32).reshape(44, 128).T)
    us = np.ascontiguousarray(np.asarray(L0_ffn_up_s, f32).reshape(44, 128).T)
    ds = np.ascontiguousarray(np.asarray(L0_ffn_down_s, f32).reshape(16, 128).T)
    sl = np.ascontiguousarray(
        np.broadcast_to(np.asarray(L0_attn_slope_rate, f32).reshape(1, 16),
                        (128, 16)))

    def tile_x(seq_slice):
        T = seq_slice.shape[0]
        a = np.ascontiguousarray(seq_slice.T)
        return np.ascontiguousarray(
            a.reshape(16, 128, T // 256, 256).transpose(2, 0, 1, 3))

    zeros_xd = np.zeros((8, 16, 128, 256), f32)
    in_maps = []
    for core in range(8):
        p, half = core // 2, core % 2
        xm = tile_x(x[p, half * 2048:(half + 1) * 2048])
        xd = tile_x(x[p, 0:2048]) if half == 1 else zeros_xd
        in_maps.append({
            "xd": xd, "xm": xm, "wq": wq, "wo": wo, "wg": wg, "wu": wu,
            "wd": wd, "gs": gs, "us": us, "ds": ds, "sl": sl,
        })
    return in_maps


def kernel(**inputs):
    in_maps = _prep_inputs(**inputs)
    nc = build_program(8, 8)
    res = run_bass_kernel_spmd(nc, in_maps, core_ids=list(range(8)))
    out = np.empty((4, 4096, 2048), np.float32)
    for core in range(8):
        p, half = core // 2, core % 2
        o = np.asarray(res.results[core]["out"])
        o = o.transpose(1, 2, 0, 3).reshape(2048, 2048)
        out[p, half * 2048:(half + 1) * 2048, :] = o.T
    return out


# version 18
# speedup vs baseline: 1.1121x; 1.1121x over previous
import contextlib
import numpy as np

import concourse.bass as bass
import concourse.mybir as mybir
from concourse.bass_utils import run_bass_kernel_spmd

F32 = mybir.dt.float32
BF16 = mybir.dt.bfloat16
AF = mybir.ActivationFunctionType
ALU = mybir.AluOpType

DM = 2048
NCH = 16
HEADS = 16
BLK = 256
INTER = 5632
ICH = 44
EPS_LN = 1e-5
EPS_RMS = 1e-6
MASK_NEG = -1.0e5


class Ledger:
    def __init__(self, nc, es, names):
        self.nc = nc
        self.h = {}
        self.c = {}
        self.lw = {}
        for n in names:
            self.h[n] = es.enter_context(nc.semaphore(n))
            self.c[n] = 0

    def inc(self, instr, name, k):
        instr.then_inc(self.h[name], k)
        self.c[name] += k
        return (name, self.c[name])

    def wait(self, eng, engname, *marks):
        for m in marks:
            if m is None:
                continue
            if isinstance(m, list):
                self.wait(eng, engname, *m)
                continue
            name, val = m
            if val <= 0:
                continue
            key = (engname, name)
            if self.lw.get(key, 0) >= val:
                continue
            eng.wait_ge(self.h[name], val)
            self.lw[key] = val

    def cur(self, name):
        return (name, self.c[name])


def build_program(n_dup=8, n_main=8):
    nc = bass.Bass("TRN2", target_bir_lowering=False, debug=False)

    xd_d = nc.dram_tensor("xd", [n_dup, NCH, 128, BLK], F32, kind="ExternalInput").ap()
    xm_d = nc.dram_tensor("xm", [n_main, NCH, 128, BLK], F32, kind="ExternalInput").ap()
    wq_d = nc.dram_tensor("wq", [64, 128, 2048], F32, kind="ExternalInput").ap()
    wo_d = nc.dram_tensor("wo", [16, 128, 2048], F32, kind="ExternalInput").ap()
    wg_d = nc.dram_tensor("wg", [44, 128, 2048], F32, kind="ExternalInput").ap()
    wu_d = nc.dram_tensor("wu", [44, 128, 2048], F32, kind="ExternalInput").ap()
    wd_d = nc.dram_tensor("wd", [16, 128, 5632], F32, kind="ExternalInput").ap()
    gs_d = nc.dram_tensor("gs", [128, 44], F32, kind="ExternalInput").ap()
    us_d = nc.dram_tensor("us", [128, 44], F32, kind="ExternalInput").ap()
    ds_d = nc.dram_tensor("ds", [128, 16], F32, kind="ExternalInput").ap()
    sl_d = nc.dram_tensor("sl", [128, 16], F32, kind="ExternalInput").ap()
    out_d = nc.dram_tensor("out", [n_main, NCH, 128, BLK], F32, kind="ExternalOutput").ap()

    wql_d = nc.dram_tensor("wql", [64, 128, 4096], BF16, kind="Internal").ap()
    wol_d = nc.dram_tensor("wol", [16, 128, 4096], BF16, kind="Internal").ap()
    wgb_d = nc.dram_tensor("wgb", [44, 128, 2048], BF16, kind="Internal").ap()
    wub_d = nc.dram_tensor("wub", [44, 128, 2048], BF16, kind="Internal").ap()
    wdb_d = nc.dram_tensor("wdb", [16, 128, 5632], BF16, kind="Internal").ap()

    import ml_dtypes
    ident_d = nc.inline_tensor(np.eye(128, dtype=np.float32), name="c_ident").ap()
    ones_col_d = nc.inline_tensor(np.ones((128, 1), np.float32), name="c_onesc").ap()
    ones_row_d = nc.inline_tensor(np.ones((1, 128), np.float32), name="c_onesr").ap()
    mi = np.full((128, 512), MASK_NEG, np.float32)
    for j in range(2):
        s = (np.arange(128) + j * 128)[:, None]
        m = np.arange(256)[None, :]
        v = (s - m).astype(np.float32)
        mi[:, j * 256:(j + 1) * 256] = np.where(m >= s, v, MASK_NEG)
    miota_d = nc.inline_tensor(mi.astype(ml_dtypes.bfloat16), name="c_miota").ap()
    negm1 = (-(np.arange(256, dtype=np.float32) + 1.0)).reshape(1, 256)
    negk = (np.arange(256, dtype=np.float32) - 255.0).reshape(1, 256)
    negm1_d = nc.inline_tensor(negm1.astype(ml_dtypes.bfloat16), name="c_negm1").ap()
    negk_d = nc.inline_tensor(negk.astype(ml_dtypes.bfloat16), name="c_negk").ap()
    neg256_d = nc.inline_tensor(np.full((128, 1), -256.0, np.float32), name="c_n256").ap()
    a2 = np.stack([np.arange(128) - 255.0, np.arange(128) - 127.0],
                  axis=1).astype(np.float32)
    a2_d = nc.inline_tensor(a2, name="c_a2").ap()

    es = contextlib.ExitStack()
    sb = lambda name, shape, dt=F32: es.enter_context(nc.sbuf_tensor(name, shape, dt))
    psum = lambda name: es.enter_context(nc.psum_tensor(name, [128, 512], F32))

    ident = sb("ident", [128, 128])
    ones_col = sb("ones_col", [128, 1])
    ones_row = sb("ones_row", [1, 128])
    miota = sb("miota", [128, 512], BF16)
    negm1_r = sb("negm1_r", [1, 256], BF16)
    negk_r = sb("negk_r", [1, 256], BF16)
    neg256 = sb("neg256", [128, 1])
    epsln = sb("epsln", [1, 1])
    epsrms = sb("epsrms", [1, 1])
    gs_t = sb("gs_t", [128, 44])
    us_t = sb("us_t", [128, 44])
    ds_t = sb("ds_t", [128, 16])
    sl_t = sb("sl_t", [128, 16])
    bd_t = sb("bd_t", [128, 16])
    rowscr = sb("rowscr", [1, 512])
    qdbc = sb("qdbc", [128, 4096])
    kd_cols = sb("kd_cols", [128, 32])
    a2_t = sb("a2_t", [128, 2])
    diagd = sb("diagd", [128, 512])
    xarena = sb("xarena", [128, 4096])
    x1arena = sb("x1arena", [128, 4096])
    x1lo = sb("x1lo", [128, 4096], BF16)
    big = sb("big", [128, 16384])
    oarena = sb("oarena", [128, 4096])
    kvA = sb("kvA", [128, 2048])
    kvB = sb("kvB", [128, 2048])
    vts = sb("vts", [128, 256])
    kts = sb("kts", [128, 256])
    qdec = sb("qdec", [128, 256])
    kqd = sb("kqd", [128, 512])
    sq0 = sb("sq0", [128, 256])
    sq1 = sb("sq1", [128, 256])
    stage = sq0
    stageg = sq1
    rb1 = sb("rb1", [128, 512])
    m_row = sb("m_row", [1, 256])
    m2_row = sb("m2_row", [1, 256])
    var_row = sb("var_row", [1, 256])
    sd_row = sb("sd_row", [1, 256])
    R_row = sb("R_row", [1, 256])
    MR_row = sb("MR_row", [1, 256])
    warena = sb("warena", [128, 12288], BF16)

    r3b = rb1[:, 0:256]

    pA = psum("pA")
    pB = psum("pB")
    pq = psum("pq")
    ptr = psum("ptr")
    po = psum("po")
    pkv = psum("pkv")
    pstx = psum("pstx")
    pstq = psum("pstq")

    L = Ledger(nc, es, ["cl"] + ["wl%d" % i for i in range(6)]
                       + ["wr%d" % i for i in range(6)]
                       + ["xl", "mm", "ve", "se", "stA", "stB", "gp",
                          "ppsA", "ppsB", "pplA", "pplB"])

    TEN, VEC, SCA, SYN, GPS = nc.tensor, nc.vector, nc.scalar, nc.sync, nc.gpsimd

    def hi_view(tensor, col0, ncols):
        return tensor[:].bitcast(BF16)[:, 2 * col0 + 1:2 * (col0 + ncols):2]

    def xc(c):
        return xarena[:, c * 256:(c + 1) * 256]

    def x1c(c):
        return x1arena[:, c * 256:(c + 1) * 256]

    def x1hi(c):
        return hi_view(x1arena, c * 256, 256)

    def x1loc(c):
        return x1lo[:, c * 256:(c + 1) * 256]

    def oc(c):
        return oarena[:, c * 256:(c + 1) * 256]

    def qh(h):
        return big[:, h * 256:(h + 1) * 256]

    def kh(h, bset=0):
        base = 4096 if bset == 0 else 0
        return big[:, base + h * 256:base + (h + 1) * 256]

    def vh(h, bset=0):
        base = 8192 if bset == 0 else 12288
        return big[:, base + h * 256:base + (h + 1) * 256]

    def uh(h):
        return big[:, 12288 + h * 256:12288 + (h + 1) * 256]

    def x3c(c):
        return big[:, c * 256:(c + 1) * 256]

    def x3hi(c):
        return hi_view(big, c * 256, 256)

    def sgc(par):
        return big[:, 4096 + par * 256:4096 + (par + 1) * 256]

    def hdnhi(c):
        return big[:].bitcast(BF16)[:, 9216 + c * 256:9216 + (c + 1) * 256]

    def hdnlo(c):
        return big[:].bitcast(BF16)[:, 20480 + c * 256:20480 + (c + 1) * 256]

    state = {"wctr": 0, "wgoal": {}, "ready": {}, "bankw": {}, "bankr": {}, "blk": 0}
    R = state["ready"]

    def load_w(dram_ap, ncols, family, f32=False):
        i = state["wctr"]
        state["wctr"] += 1
        slot = i % 6
        if f32:
            wf = warena[:].bitcast(F32)
            dst = wf[:, slot * 1024:slot * 1024 + ncols]
        else:
            dst = warena[:, slot * 2048:slot * 2048 + ncols]
        if family is not None:
            if isinstance(family, list):
                for fk in family:
                    L.wait(SYN, "SYN", R.get(fk, R.get(("wfam", "ffn"), None)))
            else:
                L.wait(SYN, "SYN", R.get(("wfam", family), None))
        if i >= 6:
            L.wait(SYN, "SYN", ("wr%d" % slot, i // 6))
        state["wgoal"][i] = L.inc(SYN.dma_start(out=dst, in_=dram_ap),
                                  "wl%d" % slot, 16)
        return i, dst, slot * 2048

    def w_release_name(i):
        return "wr%d" % (i % 6)

    def proj_matmuls_hilo(segments, xhi_fn, xlo_fn, bank, extra_waits=(),
                          w_split=True, x_marks=None):
        for w in extra_waits:
            L.wait(TEN, "TEN", w)
        mark = None
        first = True
        last_seg = len(segments) - 1
        for si, (cid, wsl, ks, strided_hi) in enumerate(segments):
            L.wait(TEN, "TEN", state["wgoal"][cid])
            if first:
                L.wait(TEN, "TEN", state["bankr"].get(bank.name, None))
            for ki, k in enumerate(ks):
                final = (si == last_seg and ki == len(ks) - 1)
                if x_marks is not None:
                    L.wait(TEN, "TEN", x_marks[k])
                if w_split:
                    whi = wsl[:, ki * 128:(ki + 1) * 128]
                    wlo = wsl[:, 1024 + ki * 128:1024 + (ki + 1) * 128]
                    TEN.matmul(bank[:, 0:256], whi, xhi_fn(k), start=first,
                               stop=False)
                    TEN.matmul(bank[:, 0:256], whi, xlo_fn(k), start=False,
                               stop=False)
                    instr = TEN.matmul(bank[:, 0:256], wlo, xhi_fn(k), start=False,
                                       stop=final)
                else:
                    if strided_hi:
                        whi = wsl[:, 2 * ki * 128 + 1:2 * (ki + 1) * 128:2]
                    else:
                        whi = wsl[:, ki * 128:(ki + 1) * 128]
                    TEN.matmul(bank[:, 0:256], whi, xhi_fn(k), start=first,
                               stop=False)
                    instr = TEN.matmul(bank[:, 0:256], whi, xlo_fn(k), start=False,
                                       stop=final)
                first = False
                if ki == len(ks) - 1:
                    L.inc(instr, w_release_name(cid), 1)
                    if final:
                        mark = L.cur(w_release_name(cid))
        state["bankw"][bank.name] = mark
        return mark

    def bank_read_done(bank, mark):
        state["bankr"][bank.name] = [mark]

    def bank_read_add(bank, mark):
        state["bankr"].setdefault(bank.name, []).append(mark)

    with es:
        es.enter_context(nc.Block())

        for dst, src in [(ident, ident_d), (ones_col, ones_col_d),
                         (ones_row, ones_row_d), (miota, miota_d),
                         (negm1_r, negm1_d), (negk_r, negk_d), (neg256, neg256_d),
                         (a2_t, a2_d),
                         (gs_t, gs_d), (us_t, us_d), (ds_t, ds_d), (sl_t, sl_d)]:
            L.inc(GPS.dma_start(out=dst[:], in_=src[:, :]), "cl", 16)
        setup_xl = L.cur("cl")

        VEC.memset(epsln[:], EPS_LN)
        VEC.memset(epsrms[:], EPS_RMS)
        L.inc(VEC.memset(kvA[:], 0.0), "ve", 1)
        R["kv0"] = L.cur("ve")
        L.wait(VEC, "VEC", setup_xl)

        def pp_in(s):
            return oarena[:, s * 2048:s * 2048 + 1024]

        def pp_out(s):
            return oarena[:].bitcast(BF16)[:, s * 4096 + 2048:(s + 1) * 4096]

        PPC = 1024
        pp_jobs = []
        kv_order = []
        for h in range(16):
            kv_order += [16 + h, 32 + h]
        order = kv_order + list(range(16)) + list(range(48, 64))
        for fo in order:
            for pc in range(2):
                pp_jobs.append((wq_d[fo][:, pc * PPC:(pc + 1) * PPC],
                                wql_d[fo][:, pc * 2048:(pc + 1) * 2048], PPC, True,
                                ("wpp", "q", fo, pc)))
        n_kv_jobs = 64
        for fo in range(16):
            for pc in range(2):
                pp_jobs.append((wo_d[fo][:, pc * PPC:(pc + 1) * PPC],
                                wol_d[fo][:, pc * 2048:(pc + 1) * 2048], PPC, True,
                                ("wpp", "o", fo, pc)))

        for fo in range(44):
            for pc in range(2):
                pp_jobs.append((wg_d[fo][:, pc * PPC:(pc + 1) * PPC],
                                wgb_d[fo][:, pc * PPC:(pc + 1) * PPC], PPC, False,
                                ("wpp", "g", fo, pc)))
        for fo in range(44):
            for pc in range(2):
                pp_jobs.append((wu_d[fo][:, pc * PPC:(pc + 1) * PPC],
                                wub_d[fo][:, pc * PPC:(pc + 1) * PPC], PPC, False,
                                ("wpp", "u", fo, pc)))
        for ho in range(16):
            for pc in range(8):
                pp_jobs.append((wd_d[ho][:, pc * 704:(pc + 1) * 704],
                                wdb_d[ho][:, pc * 704:(pc + 1) * 704], 704, False,
                                ("wpp", "d", ho, pc)))

        pp_done = {}
        pp_state = {"next": 0}

        def pp_emit(count):
            end = min(pp_state["next"] + count, len(pp_jobs))
            for i in range(pp_state["next"], end):
                srcap, dstp, ncols, split, ckey = pp_jobs[i]
                s = i % 2
                L.wait(SYN, "SYN", pp_done.get(i - 2, None))
                ld = L.inc(SYN.dma_start(out=pp_in(s)[:, 0:ncols], in_=srcap),
                           "pplA" if s == 0 else "pplB", 16)
                hi_ap = pp_out(s)[:, 0:ncols]
                L.wait(SCA, "SCA", ld, pp_done.get(i - 2, None))
                h1 = L.inc(SCA.activation(hi_ap, pp_in(s)[:, 0:ncols], AF.Copy),
                           "se", 1)
                if split:
                    lo_ap = pp_out(s)[:, 1024:1024 + ncols]
                    L.wait(VEC, "VEC", h1, pp_done.get(i - 2, None))
                    h2 = L.inc(VEC.tensor_tensor(lo_ap, pp_in(s)[:, 0:ncols],
                                                 hi_ap, ALU.subtract), "ve", 1)
                    L.wait(GPS, "GPS", h2)
                    stm = L.inc(GPS.dma_start(out=dstp, in_=pp_out(s)[:, 0:2048]),
                                "ppsA" if s == 0 else "ppsB", 16)
                else:
                    L.wait(GPS, "GPS", h1)
                    stm = L.inc(GPS.dma_start(out=dstp, in_=pp_out(s)[:, 0:ncols]),
                                "ppsA" if s == 0 else "ppsB", 16)
                pp_done[i] = stm
                if ckey is not None:
                    R[ckey] = stm
                if i == n_kv_jobs - 1:
                    R[("wfam", "kv")] = [L.cur("ppsA"), L.cur("ppsB")]
                if i == 128 - 1:
                    R[("wfam", "qu")] = [L.cur("ppsA"), L.cur("ppsB")]
                if i == 160 - 1:
                    R[("wfam", "wo")] = [L.cur("ppsA"), L.cur("ppsB")]
                if i == len(pp_jobs) - 1:
                    R[("wfam", "ffn")] = [L.cur("ppsA"), L.cur("ppsB")]
                    R["pp_oarena_free"] = R[("wfam", "ffn")]
            pp_state["next"] = end


        L.wait(SCA, "SCA", setup_xl)
        emit_ln0 = True
        for h in range(HEADS):
            instr = SCA.activation(bd_t[:, h:h + 1], neg256[:, 0:1],
                                   AF.Exp, scale=sl_t[:, h:h + 1])
        dr_mark = L.inc(instr, "se", 1)
        for h in range(HEADS):
            instr = SCA.activation(kd_cols[:, 2 * h:2 * h + 2], a2_t[:, 0:2],
                                   AF.Exp, scale=sl_t[:, h:h + 1])
        R["kdcols"] = L.inc(instr, "se", 1)
        for h in range(HEADS):
            L.wait(SCA, "SCA", L.cur("mm"))
            se_h = L.inc(SCA.activation(rowscr[0:1, 0:256], negm1_r[0:1, :],
                                        AF.Exp, scale=sl_t[0:1, h:h + 1]), "se", 1)
            L.wait(TEN, "TEN", se_h, state["bankr"].get(pq.name, None))
            mmk = L.inc(TEN.matmul(pq[:, 0:256], ones_row[:], rowscr[0:1, 0:256],
                                   start=True, stop=True), "mm", 1)
            L.wait(VEC, "VEC", mmk)
            ev = L.inc(VEC.tensor_copy(qdbc[:, h * 256:(h + 1) * 256], pq[:, 0:256]),
                       "ve", 1)
            bank_read_done(pq, ev)
        R["qdkd"] = L.cur("ve")
        R["bd"] = dr_mark
        for h in range(HEADS):
            R[("kv_ok", 0, h)] = R["kv0"]

        def ln_stats_chunk(x_ap, c, waits=()):
            L.wait(SCA, "SCA", R.get(("sq_busy", c % 2), None), *waits)
            se = L.inc(SCA.square(sq0[:] if c % 2 == 0 else sq1[:], x_ap), "se", 1)
            L.wait(TEN, "TEN", se)
            if c == 0:
                L.wait(TEN, "TEN", state["bankr"].get(pstx.name, None),
                       state["bankr"].get(pstq.name, None))
            TEN.matmul(pstx[0:1, 0:256], ones_col[:], x_ap,
                       start=(c == 0), stop=(c == NCH - 1))
            instr = TEN.matmul(pstq[0:1, 0:256], ones_col[:],
                               sq0[:] if c % 2 == 0 else sq1[:],
                               start=(c == 0), stop=(c == NCH - 1))
            R[("sq_busy", c % 2)] = L.inc(instr, "mm", 1)

        def layernorm(x_chunk_fn, dst_fn, dstlo_fn, rb, src_waits, dst_waits=None,
                      split_apply=True):
            for c in range(NCH):
                if c == 0:
                    L.wait(SCA, "SCA", src_waits)
                    L.wait(TEN, "TEN", src_waits)
                ln_stats_chunk(x_chunk_fn(c), c)
            return ln_finish(x_chunk_fn, dst_fn, dstlo_fn, rb, dst_waits,
                             split_apply)

        def ln_finish(x_chunk_fn, dst_fn, dstlo_fn, rb, dst_waits=None,
                      split_apply=True):
            mm_stats = L.cur("mm")
            L.wait(SCA, "SCA", mm_stats)
            cp = L.inc(SCA.activation(m_row[0:1, :], pstx[0:1, 0:256], AF.Copy,
                                      scale=1.0 / DM), "se", 1)
            L.wait(SCA, "SCA", cp)
            se1 = L.inc(SCA.square(m2_row[0:1, :], m_row[0:1, :]), "se", 1)
            bank_read_done(pstx, se1)
            L.wait(VEC, "VEC", se1)
            ve1 = L.inc(VEC.scalar_tensor_tensor(var_row[0:1, :], pstq[0:1, 0:256],
                                                 1.0 / DM, m2_row[0:1, :],
                                                 ALU.mult, ALU.subtract), "ve", 1)
            bank_read_done(pstq, ve1)
            L.wait(SCA, "SCA", ve1)
            se2 = L.inc(SCA.activation(sd_row[0:1, :], var_row[0:1, :], AF.Sqrt,
                                       bias=epsln[0:1, 0:1]), "se", 1)
            L.wait(VEC, "VEC", se2)
            rv = L.inc(VEC.reciprocal(R_row[0:1, :], sd_row[0:1, :]), "ve", 1)
            L.wait(VEC, "VEC", rv)
            ve2 = L.inc(VEC.tensor_mul(MR_row[0:1, :], m_row[0:1, :], R_row[0:1, :]),
                        "ve", 1)
            L.wait(TEN, "TEN", ve2, state["bankr"].get(pq.name, None))
            TEN.matmul(pq[:, 0:256], ones_row[:], R_row[0:1, :], start=True, stop=True)
            instr = TEN.matmul(pq[:, 256:512], ones_row[:], MR_row[0:1, :],
                               start=True, stop=True)
            mmb = L.inc(instr, "mm", 1)
            L.wait(VEC, "VEC", mmb)
            ve3 = L.inc(VEC.tensor_copy(rb[:, 0:512], pq[:, 0:512]), "ve", 1)
            bank_read_done(pq, ve3)
            L.wait(VEC, "VEC", ve3)
            L.wait(GPS, "GPS", ve3)
            chunk_marks = []
            for c in range(NCH):
                use_gps = split_apply and (c % 3 == 2)
                eng, en, sem, stg2 = ((GPS, "GPS", "gp", stageg) if use_gps
                                      else (VEC, "VEC", "ve", stage))
                if dst_waits is not None:
                    L.wait(eng, en, dst_waits(c))
                m1 = L.inc(eng.tensor_tensor(stg2[:], x_chunk_fn(c), rb[:, 0:256],
                                             ALU.mult), sem, 1)
                L.wait(eng, en, m1)
                m2 = L.inc(eng.tensor_tensor(dst_fn(c), stg2[:], rb[:, 256:512],
                                             ALU.subtract), sem, 1)
                L.wait(eng, en, m2)
                dst_ap = dst_fn(c)
                m3 = L.inc(eng.tensor_tensor(dstlo_fn(c), dst_ap,
                                             dst_ap.bitcast(BF16)[:, 1::2],
                                             ALU.subtract), sem, 1)
                L.wait(eng, en, m3)
                chunk_marks.append(m3)
            return chunk_marks

        def attn_A(h, dup, bset=0):
            if not dup:
                L.wait(SCA, "SCA", R.get(("diagd_free",), None))
                sed = L.inc(SCA.activation(diagd[:], miota[:], AF.Exp,
                                           scale=sl_t[:, h:h + 1]), "se", 1)
                L.wait(TEN, "TEN", state["bankr"].get(pq.name, None),
                       R[("q", h)], R[("k", h, 0)])
                TEN.matmul(pq[:, 0:256], kh(h)[:, 0:128], qh(h), start=True, stop=True)
                mmq = L.inc(TEN.matmul(pq[:, 256:512], kh(h)[:, 128:256], qh(h),
                                       start=True, stop=True), "mm", 1)
                state["bankw"][pq.name] = mmq
                R[("mmq", h)] = mmq
            if not dup:
                L.wait(VEC, "VEC", R[("mmq", h)], sed, R.get(("kqd_free",), None))
                VEC.tensor_tensor(kqd[:, 0:256], pq[:, 0:256], diagd[:, 0:256],
                                  ALU.mult)
                vek = L.inc(VEC.tensor_tensor(kqd[:, 256:512], pq[:, 256:512],
                                              diagd[:, 256:512], ALU.mult), "ve", 1)
                bank_read_done(pq, vek)
                R[("diagd_free",)] = vek
                R[("vek", h)] = vek
                L.wait(VEC, "VEC", R[("q", h)], R["qdkd"],
                       R.get(("qdec_free",), None))
                veq = L.inc(VEC.tensor_tensor(qdec[:], qh(h),
                                              qdbc[:, h * 256:(h + 1) * 256],
                                              ALU.mult), "ve", 1)
                R[("veq", h)] = veq
                R[("q_used", h)] = [R[("mmq", h)], veq]
            L.wait(TEN, "TEN", state["bankr"].get(ptr.name, None),
                   R[("v", h, bset)], R[("k", h, bset)])
            TEN.transpose(ptr[:, 0:128], vh(h, bset)[:, 0:128], ident[:])
            TEN.transpose(ptr[:, 128:256], vh(h, bset)[:, 128:256], ident[:])
            TEN.transpose(ptr[:, 256:384], kh(h, bset)[:, 0:128], ident[:])
            mmt = L.inc(TEN.transpose(ptr[:, 384:512], kh(h, bset)[:, 128:256],
                                      ident[:]), "mm", 1)
            R[("v_used", h, bset)] = mmt
            L.wait(VEC, "VEC", mmt, R["kdcols"], R.get(("vts_free",), None),
                   R.get(("outbuf", 0), None), R.get(("outbuf", 1), None))
            VEC.tensor_copy(vts[:], ptr[:, 0:256])
            VEC.tensor_scalar(kts[:, 0:128], ptr[:, 256:384],
                              kd_cols[:, 2 * h:2 * h + 1], None, ALU.mult)
            vet = L.inc(VEC.tensor_scalar(kts[:, 128:256], ptr[:, 384:512],
                                          kd_cols[:, 2 * h + 1:2 * h + 2], None,
                                          ALU.mult), "ve", 1)
            bank_read_done(ptr, vet)
            R[("vet", h)] = vet
            R[("k_used", h, bset)] = [mmt] + ([R[("mmq", h)]] if not dup else [])

        def attn_B(h, dup, first_main, blk=None):
            parity = (state["blk"] if blk is None else blk) % 2
            kv_cur = kvA if parity == 0 else kvB
            kv_nxt = kvB if parity == 0 else kvA
            kvs_cur = kv_cur[:, h * 128:(h + 1) * 128]
            kvs_nxt = kv_nxt[:, h * 128:(h + 1) * 128]

            if not dup:
                L.wait(TEN, "TEN", state["bankr"].get(po.name, None),
                       R[("vet", h)], R[("vek", h)], R[("veq", h)],
                       R[("kv_ok", parity, h)])
                TEN.matmul(po[:, 0:256], vts[:, 0:128], kqd[:, 0:256],
                           start=True, stop=False)
                TEN.matmul(po[:, 0:256], vts[:, 128:256], kqd[:, 256:512],
                           start=False, stop=False)
                mmo = L.inc(TEN.matmul(po[:, 0:256], kvs_cur, qdec[:],
                                       start=False, stop=True), "mm", 1)
                state["bankw"][po.name] = mmo
                R[("kv_read", parity, h)] = mmo
                R[("kqd_free",)] = mmo
                R[("qdec_free",)] = mmo
            else:
                L.wait(TEN, "TEN", R[("vet", h)])
                mmo = None
            L.wait(TEN, "TEN", state["bankr"].get(pkv.name, None))
            TEN.matmul(pkv[:, 0:128], kts[:, 0:128], vts[:, 0:128],
                       start=True, stop=False)
            mmk2 = L.inc(TEN.matmul(pkv[:, 0:128], kts[:, 128:256], vts[:, 128:256],
                                    start=False, stop=True), "mm", 1)
            R[("vts_free",)] = mmk2
            if not dup and h >= 1:
                sqb_prev = sq0 if (h - 1) % 2 == 0 else sq1
                L.wait(TEN, "TEN", R[("seo", h - 1)])
                if h - 1 == 0:
                    L.wait(TEN, "TEN", state["bankr"].get(pstx.name, None))
                instr = TEN.matmul(pstx[0:1, 0:256], ones_col[:], sqb_prev[:],
                                   start=(h - 1 == 0), stop=False)
                R[("o2_free", (h - 1) % 2)] = L.inc(instr, "mm", 1)
            L.wait(VEC, "VEC", mmk2, R["bd"], R[("kv_ok", parity, h)],
                   R.get(("kv_read", parity, h), None),
                   R.get(("kv_read", 1 - parity, h), None))
            vkv = L.inc(VEC.scalar_tensor_tensor(kvs_nxt, kvs_cur, bd_t[:, h:h + 1],
                                                 pkv[:, 0:128], ALU.mult, ALU.add),
                        "ve", 1)
            bank_read_done(pkv, vkv)
            R[("kv_ok", 1 - parity, h)] = vkv

            if not dup:
                L.wait(VEC, "VEC", mmo)
                if first_main and h == 0:
                    L.wait(VEC, "VEC", R["pp_oarena_free"])
                veo = L.inc(VEC.tensor_copy(oc(h), po[:, 0:256]), "ve", 1)
                sqb = sq0 if h % 2 == 0 else sq1
                L.wait(SCA, "SCA", mmo, R.get(("o2_free", h % 2), None))
                seo = L.inc(SCA.square(sqb[:], po[:, 0:256]), "se", 1)
                R[("seo", h)] = seo
                bank_read_done(po, veo)
                bank_read_add(po, seo)

        def emit_x_load(t, dup):
            x_dram = xd_d if dup else xm_d
            L.wait(GPS, "GPS", R.get("x_free", None))
            xin = xarena[:].rearrange("p (c m) -> p c m", c=NCH)
            return L.inc(GPS.dma_start(out=xin,
                                       in_=x_dram[t].rearrange("c p m -> p c m")),
                         "xl", 16)

        def emit_tile(t, dup, first_main=False):

            pre = R.pop(("x1_ready", dup, t), None)
            if pre is not None:
                x1_ready = pre
            else:
                key = ("tile_xl", dup, t)
                tile_xl = R.pop(key, None) or emit_x_load(t, dup)
                x1_ready = layernorm(xc, x1c, x1loc, rb1, tile_xl,
                                     dst_waits=lambda c: R.get("x1_free", None))
            R["x1"] = x1_ready

            banks = [pA, pB]
            pbanks = [pA, pB, pq, po] if dup else [pA, pB]
            pidx = [0]

            def proj_chunk(fo, bset=0):
                segs = []
                for half in range(2):
                    cid, wsl, _ = load_w(wql_d[fo][:, half * 2048:(half + 1) * 2048],
                                         2048, [("wpp", "q", fo, half)])
                    segs.append((cid, wsl, list(range(half * 8, half * 8 + 8)), False))
                bank = pbanks[pidx[0] % len(pbanks)]
                pidx[0] += 1
                mark = proj_matmuls_hilo(segs, x1hi, x1loc, bank,
                                         x_marks=x1_ready)
                role, hh = divmod(fo, 16)
                if role == 0:
                    L.wait(SCA, "SCA", mark, R.get(("q_used", hh), None))
                    ev = L.inc(SCA.activation(qh(hh), bank[:, 0:256], AF.Silu),
                               "se", 1)
                    R[("q", hh)] = ev
                elif role == 1:
                    L.wait(SCA, "SCA", mark, R.get(("k_used", hh, bset), None))
                    ev = L.inc(SCA.activation(kh(hh, bset), bank[:, 0:256],
                                              AF.Silu), "se", 1)
                    R[("k", hh, bset)] = ev
                elif role == 2:
                    L.wait(SCA, "SCA", mark, R.get(("v_used", hh, bset), None))
                    ev = L.inc(SCA.copy(vh(hh, bset), bank[:, 0:256]), "se", 1)
                    R[("v", hh, bset)] = ev
                else:
                    L.wait(SCA, "SCA", mark, R.get(("u_used", hh), None))
                    ev = L.inc(SCA.copy(uh(hh), bank[:, 0:256]), "se", 1)
                    R[("u", hh)] = ev
                bank_read_done(bank, ev)
                R["last_proj_mark"] = mark

            if dup:
                bset = t % 2
                prev = t - 1
                for h in range(HEADS):
                    for role in (1, 2):
                        proj_chunk(role * 16 + h, bset=bset)
                        pp_emit(2)
                    if prev >= 0:
                        if h >= 1:
                            attn_B(h - 1, True, False, blk=prev)
                        attn_A(h, True, bset=prev % 2)
                if prev >= 0:
                    attn_B(HEADS - 1, True, False, blk=prev)
                R["x1_free"] = R["last_proj_mark"]
                R["x_free"] = x1_ready
                if t + 1 < n_dup:
                    xl_next = emit_x_load(t + 1, True)
                    R[("x1_ready", True, t + 1)] = layernorm(
                        xc, x1c, x1loc, rb1, xl_next,
                        dst_waits=lambda c: R.get("x1_free", None))
                else:
                    for h in range(HEADS):
                        if h >= 1:
                            attn_B(h - 1, True, False, blk=t)
                        attn_A(h, True, bset=t % 2)
                    attn_B(HEADS - 1, True, False, blk=t)
                    if n_main > 0:
                        xl_next = emit_x_load(0, False)
                        R[("x1_ready", False, 0)] = layernorm(
                            xc, x1c, x1loc, rb1, xl_next,
                            dst_waits=lambda c: R.get("x1_free", None))
                state["blk"] += 1
                return

            roles = (0, 1, 2, 3)
            for h in range(HEADS):
                for role in roles:
                    proj_chunk(role * 16 + h)
                if h >= 1:
                    attn_B(h - 1, dup, first_main)
                attn_A(h, dup)
            attn_B(HEADS - 1, dup, first_main)
            if not dup:
                hh = HEADS - 1
                sqb_prev = sq0 if hh % 2 == 0 else sq1
                L.wait(TEN, "TEN", R[("seo", hh)])
                instr = TEN.matmul(pstx[0:1, 0:256], ones_col[:], sqb_prev[:],
                                   start=False, stop=True)
                R[("o2_free", hh % 2)] = L.inc(instr, "mm", 1)
            state["blk"] += 1

            mm_rms = L.cur("mm")
            L.wait(SCA, "SCA", mm_rms)
            se1 = L.inc(SCA.activation(m_row[0:1, :], pstx[0:1, 0:256], AF.Copy,
                                       scale=1.0 / DM), "se", 1)
            bank_read_done(pstx, se1)
            L.wait(SCA, "SCA", se1)
            se2 = L.inc(SCA.activation(sd_row[0:1, :], m_row[0:1, :], AF.Sqrt,
                                       bias=epsrms[0:1, 0:1]), "se", 1)
            L.wait(VEC, "VEC", se2)
            ve1 = L.inc(VEC.reciprocal(R_row[0:1, :], sd_row[0:1, :]), "ve", 1)
            L.wait(TEN, "TEN", ve1, state["bankr"].get(pq.name, None))
            mmb = L.inc(TEN.matmul(pq[:, 0:256], ones_row[:], R_row[0:1, :],
                                   start=True, stop=True), "mm", 1)
            L.wait(VEC, "VEC", mmb)
            ve2 = L.inc(VEC.tensor_copy(r3b, pq[:, 0:256]), "ve", 1)
            bank_read_done(pq, ve2)

            L.wait(VEC, "VEC", R[("u", NCH - 1)])
            L.wait(VEC, "VEC", ve2)
            L.wait(GPS, "GPS", R[("u", NCH - 1)], ve2)
            x2in_marks = []
            for c in range(NCH):
                use_gps = (c % 3 == 2)
                eng, en, sem, stg2 = ((GPS, "GPS", "gp", stageg) if use_gps
                                      else (VEC, "VEC", "ve", stage))
                L.wait(eng, en, R[("u", c)])
                m1 = L.inc(eng.tensor_tensor(stg2[:], oc(c), r3b, ALU.mult),
                           sem, 1)
                L.wait(eng, en, m1)
                m2 = L.inc(eng.tensor_tensor(x1c(c), stg2[:], uh(c), ALU.mult),
                           sem, 1)
                L.wait(eng, en, m2)
                m3 = L.inc(eng.tensor_tensor(x1loc(c), x1c(c), x1hi(c),
                                             ALU.subtract), sem, 1)
                L.wait(eng, en, m3)
                x2in_marks.append(m3)
            x2in_ready = [x2in_marks[NCH - 1], x2in_marks[NCH - 2]]
            for c in range(NCH):
                R[("u_used", c)] = x2in_ready

            x_free_marks = []
            for fo in range(NCH):
                segs = []
                for half in range(2):
                    cid, wsl, _ = load_w(wol_d[fo][:, half * 2048:(half + 1) * 2048],
                                         2048, [("wpp", "o", fo, half)])
                    segs.append((cid, wsl, list(range(half * 8, half * 8 + 8)), False))
                bank = banks[fo % 2]
                mark = proj_matmuls_hilo(segs, x1hi, x1loc, bank,
                                         x_marks=x2in_marks)
                L.wait(VEC, "VEC", mark)
                ev = L.inc(VEC.tensor_tensor(oc(fo), bank[:, 0:256], xc(fo),
                                             ALU.add), "ve", 1)
                bank_read_done(bank, ev)
                x_free_marks.append(ev)
            R["x_free"] = x_free_marks[-1]
            R["x2"] = x_free_marks
            R["x1_free"] = x_free_marks[-1]

            x3_ready = layernorm(oc, x3c, x1loc, rb1, R["x2"],
                                 dst_waits=lambda c: [R.get(("q_used", c), None),
                                                      R["x1_free"]])

            if t + 1 < n_main:
                R[("tile_xl", False, t + 1)] = emit_x_load(t + 1, False)

            for fo in range(ICH):
                par = fo % 2
                cidg, wslg, _ = load_w(wgb_d[fo], 2048, [("wpp", "g", fo, 0), ("wpp", "g", fo, 1)])
                markg = proj_matmuls_hilo([(cidg, wslg, list(range(NCH)), False)],
                                          x3hi, x1loc, pA,
                                          w_split=False, x_marks=x3_ready)
                L.wait(SCA, "SCA", markg, R.get(("sg_used", par), None))
                seg = L.inc(SCA.activation(sgc(par), pA[:, 0:256], AF.Silu,
                                           scale=gs_t[:, fo:fo + 1]), "se", 1)
                bank_read_done(pA, seg)
                cidu, wslu, _ = load_w(wub_d[fo], 2048, [("wpp", "u", fo, 0), ("wpp", "u", fo, 1)])
                marku = proj_matmuls_hilo([(cidu, wslu, list(range(NCH)), False)],
                                          x3hi, x1loc, pB,
                                          w_split=False, x_marks=x3_ready)
                L.wait(VEC, "VEC", marku, seg)
                n1 = L.inc(VEC.scalar_tensor_tensor(stage[:], pB[:, 0:256],
                                                    us_t[:, fo:fo + 1], sgc(par),
                                                    ALU.mult, ALU.mult), "ve", 1)
                bank_read_done(pB, n1)
                L.wait(VEC, "VEC", n1)
                n2 = L.inc(VEC.tensor_copy(hdnhi(fo), stage[:]), "ve", 1)
                L.wait(VEC, "VEC", n2)
                n3 = L.inc(VEC.tensor_tensor(hdnlo(fo), stage[:], hdnhi(fo),
                                             ALU.subtract), "ve", 1)
                L.wait(VEC, "VEC", n3)
                R[("sg_used", par)] = n1
            hdn_ready = L.cur("ve")

            if t + 1 < n_main:
                xl_next = R.pop(("tile_xl", False, t + 1))
                R[("x1_ready", False, t + 1)] = layernorm(
                    xc, x1c, x1loc, rb1, [xl_next, hdn_ready],
                    dst_waits=lambda c: R.get("x1_free", None))

            for ho in range(NCH):
                bank = banks[ho % 2]
                for quarter in range(4):
                    cid, wsl, _ = load_w(
                        wdb_d[ho][:, quarter * 1408:(quarter + 1) * 1408],
                        1408, [("wpp", "d", ho, 2 * quarter),
                               ("wpp", "d", ho, 2 * quarter + 1)])
                    L.wait(TEN, "TEN", state["wgoal"][cid])
                    if quarter == 0:
                        L.wait(TEN, "TEN", state["bankr"].get(bank.name, None),
                               hdn_ready)
                    for k in range(11):
                        kk = quarter * 11 + k
                        TEN.matmul(bank[:, 0:256], wsl[:, k * 128:(k + 1) * 128],
                                   hdnhi(kk), start=(kk == 0), stop=False)
                        instr = TEN.matmul(bank[:, 0:256],
                                           wsl[:, k * 128:(k + 1) * 128],
                                           hdnlo(kk), start=False,
                                           stop=(kk == ICH - 1 and k == 10))
                    mark = L.inc(instr, w_release_name(cid), 1)
                state["bankw"][bank.name] = mark
                par = ho % 2
                obuf = vts if par == 0 else kts
                L.wait(VEC, "VEC", mark, R.get(("outbuf", par), None))
                ev = L.inc(VEC.scalar_tensor_tensor(obuf[:],
                                                    bank[:, 0:256], ds_t[:, ho:ho + 1],
                                                    oc(ho), ALU.mult, ALU.add),
                           "ve", 1)
                bank_read_done(bank, ev)
                L.wait(GPS, "GPS", ev)
                stm = L.inc(GPS.dma_start(out=out_d[t][ho], in_=obuf[:]),
                            "stA" if par == 0 else "stB", 16)
                R[("outbuf", par)] = stm

        if n_dup > 0:
            xl0 = emit_x_load(0, True)
            R[("x1_ready", True, 0)] = layernorm(
                xc, x1c, x1loc, rb1, xl0, dst_waits=None)
        pp_emit(n_kv_jobs)

        for t in range(n_dup):
            emit_tile(t, dup=True)
        pp_emit(len(pp_jobs))
        for t in range(n_main):
            emit_tile(t, dup=False, first_main=(t == 0))

        GPS.wait_ge(L.h["stA"], L.c["stA"])
        GPS.wait_ge(L.h["stB"], L.c["stB"])

    return nc


def _prep_inputs(L0_attn_x, L0_attn_slope_rate, L0_attn_W_qkvu, L0_attn_W_out,
                 L0_ffn_gate_w, L0_ffn_up_w, L0_ffn_down_w, L0_ffn_gate_s,
                 L0_ffn_up_s, L0_ffn_down_s):
    f32 = np.float32
    x = np.ascontiguousarray(L0_attn_x, dtype=f32)
    wq = np.ascontiguousarray(
        np.asarray(L0_attn_W_qkvu, f32).reshape(64, 128, 16, 128)
        .transpose(0, 3, 2, 1).reshape(64, 128, 2048))
    wo = np.ascontiguousarray(
        np.asarray(L0_attn_W_out, f32).reshape(16, 128, 16, 128)
        .transpose(0, 3, 2, 1).reshape(16, 128, 2048))
    wg = np.ascontiguousarray(
        np.asarray(L0_ffn_gate_w, f32).reshape(44, 128, 16, 128)
        .transpose(0, 3, 2, 1).reshape(44, 128, 2048))
    wu = np.ascontiguousarray(
        np.asarray(L0_ffn_up_w, f32).reshape(44, 128, 16, 128)
        .transpose(0, 3, 2, 1).reshape(44, 128, 2048))
    wd = np.ascontiguousarray(
        np.asarray(L0_ffn_down_w, f32).reshape(16, 128, 44, 128)
        .transpose(0, 3, 2, 1).reshape(16, 128, 5632))
    gs = np.ascontiguousarray(np.asarray(L0_ffn_gate_s, f32).reshape(44, 128).T)
    us = np.ascontiguousarray(np.asarray(L0_ffn_up_s, f32).reshape(44, 128).T)
    ds = np.ascontiguousarray(np.asarray(L0_ffn_down_s, f32).reshape(16, 128).T)
    sl = np.ascontiguousarray(
        np.broadcast_to(np.asarray(L0_attn_slope_rate, f32).reshape(1, 16),
                        (128, 16)))

    def tile_x(seq_slice):
        T = seq_slice.shape[0]
        a = np.ascontiguousarray(seq_slice.T)
        return np.ascontiguousarray(
            a.reshape(16, 128, T // 256, 256).transpose(2, 0, 1, 3))

    zeros_xd = np.zeros((8, 16, 128, 256), f32)
    in_maps = []
    for core in range(8):
        p, half = core // 2, core % 2
        xm = tile_x(x[p, half * 2048:(half + 1) * 2048])
        xd = tile_x(x[p, 0:2048]) if half == 1 else zeros_xd
        in_maps.append({
            "xd": xd, "xm": xm, "wq": wq, "wo": wo, "wg": wg, "wu": wu,
            "wd": wd, "gs": gs, "us": us, "ds": ds, "sl": sl,
        })
    return in_maps


def kernel(**inputs):
    in_maps = _prep_inputs(**inputs)
    nc = build_program(8, 8)
    res = run_bass_kernel_spmd(nc, in_maps, core_ids=list(range(8)))
    out = np.empty((4, 4096, 2048), np.float32)
    for core in range(8):
        p, half = core // 2, core % 2
        o = np.asarray(res.results[core]["out"])
        o = o.transpose(1, 2, 0, 3).reshape(2048, 2048)
        out[p, half * 2048:(half + 1) * 2048, :] = o.T
    return out


# version 21 (final)
# speedup vs baseline: 1.1427x; 1.0275x over previous
import contextlib
import numpy as np

import concourse.bass as bass
import concourse.mybir as mybir
from concourse.bass_utils import run_bass_kernel_spmd

F32 = mybir.dt.float32
BF16 = mybir.dt.bfloat16
AF = mybir.ActivationFunctionType
ALU = mybir.AluOpType

DM = 2048
NCH = 16
HEADS = 16
BLK = 256
INTER = 5632
ICH = 44
EPS_LN = 1e-5
EPS_RMS = 1e-6
MASK_NEG = -1.0e5


class Ledger:
    def __init__(self, nc, es, names):
        self.nc = nc
        self.h = {}
        self.c = {}
        self.lw = {}
        for n in names:
            self.h[n] = es.enter_context(nc.semaphore(n))
            self.c[n] = 0

    def inc(self, instr, name, k):
        instr.then_inc(self.h[name], k)
        self.c[name] += k
        return (name, self.c[name])

    def wait(self, eng, engname, *marks):
        for m in marks:
            if m is None:
                continue
            if isinstance(m, list):
                self.wait(eng, engname, *m)
                continue
            name, val = m
            if val <= 0:
                continue
            key = (engname, name)
            if self.lw.get(key, 0) >= val:
                continue
            eng.wait_ge(self.h[name], val)
            self.lw[key] = val

    def cur(self, name):
        return (name, self.c[name])


def build_program(n_dup=8, n_main=8):
    nc = bass.Bass("TRN2", target_bir_lowering=False, debug=False)

    xd_d = nc.dram_tensor("xd", [n_dup, NCH, 128, BLK], F32, kind="ExternalInput").ap()
    xm_d = nc.dram_tensor("xm", [n_main, NCH, 128, BLK], F32, kind="ExternalInput").ap()
    wq_d = nc.dram_tensor("wq", [64, 128, 2048], F32, kind="ExternalInput").ap()
    wo_d = nc.dram_tensor("wo", [16, 128, 2048], F32, kind="ExternalInput").ap()
    wg_d = nc.dram_tensor("wg", [44, 128, 2048], F32, kind="ExternalInput").ap()
    wu_d = nc.dram_tensor("wu", [44, 128, 2048], F32, kind="ExternalInput").ap()
    wd_d = nc.dram_tensor("wd", [16, 128, 5632], F32, kind="ExternalInput").ap()
    gs_d = nc.dram_tensor("gs", [128, 44], F32, kind="ExternalInput").ap()
    us_d = nc.dram_tensor("us", [128, 44], F32, kind="ExternalInput").ap()
    ds_d = nc.dram_tensor("ds", [128, 16], F32, kind="ExternalInput").ap()
    sl_d = nc.dram_tensor("sl", [128, 16], F32, kind="ExternalInput").ap()
    out_d = nc.dram_tensor("out", [n_main, NCH, 128, BLK], F32, kind="ExternalOutput").ap()

    wql_d = nc.dram_tensor("wql", [64, 128, 4096], BF16, kind="Internal").ap()
    wol_d = nc.dram_tensor("wol", [16, 128, 4096], BF16, kind="Internal").ap()
    wgb_d = nc.dram_tensor("wgb", [44, 128, 2048], BF16, kind="Internal").ap()
    wub_d = nc.dram_tensor("wub", [44, 128, 2048], BF16, kind="Internal").ap()
    wdb_d = nc.dram_tensor("wdb", [16, 128, 5632], BF16, kind="Internal").ap()

    import ml_dtypes
    ident_d = nc.inline_tensor(np.eye(128, dtype=np.float32), name="c_ident").ap()
    ones_col_d = nc.inline_tensor(np.ones((128, 1), np.float32), name="c_onesc").ap()
    ones_row_d = nc.inline_tensor(np.ones((1, 128), np.float32), name="c_onesr").ap()
    mi = np.full((128, 512), MASK_NEG, np.float32)
    for j in range(2):
        s = (np.arange(128) + j * 128)[:, None]
        m = np.arange(256)[None, :]
        v = (s - m).astype(np.float32)
        mi[:, j * 256:(j + 1) * 256] = np.where(m >= s, v, MASK_NEG)
    miota_d = nc.inline_tensor(mi.astype(ml_dtypes.bfloat16), name="c_miota").ap()
    negm1 = (-(np.arange(256, dtype=np.float32) + 1.0)).reshape(1, 256)
    negk = (np.arange(256, dtype=np.float32) - 255.0).reshape(1, 256)
    negm1_d = nc.inline_tensor(negm1.astype(ml_dtypes.bfloat16), name="c_negm1").ap()
    negk_d = nc.inline_tensor(negk.astype(ml_dtypes.bfloat16), name="c_negk").ap()
    neg256_d = nc.inline_tensor(np.full((128, 1), -256.0, np.float32), name="c_n256").ap()
    a2 = np.stack([np.arange(128) - 255.0, np.arange(128) - 127.0],
                  axis=1).astype(np.float32)
    a2_d = nc.inline_tensor(a2, name="c_a2").ap()

    es = contextlib.ExitStack()
    sb = lambda name, shape, dt=F32: es.enter_context(nc.sbuf_tensor(name, shape, dt))
    psum = lambda name: es.enter_context(nc.psum_tensor(name, [128, 512], F32))

    ident = sb("ident", [128, 128])
    ones_col = sb("ones_col", [128, 1])
    ones_row = sb("ones_row", [1, 128])
    miota = sb("miota", [128, 512], BF16)
    negm1_r = sb("negm1_r", [1, 256], BF16)
    negk_r = sb("negk_r", [1, 256], BF16)
    neg256 = sb("neg256", [128, 1])
    epsln = sb("epsln", [1, 1])
    epsrms = sb("epsrms", [1, 1])
    gs_t = sb("gs_t", [128, 44])
    us_t = sb("us_t", [128, 44])
    ds_t = sb("ds_t", [128, 16])
    sl_t = sb("sl_t", [128, 16])
    bd_t = sb("bd_t", [128, 16])
    rowscr = sb("rowscr", [1, 512])
    qdbc = sb("qdbc", [128, 4096])
    kd_cols = sb("kd_cols", [128, 32])
    a2_t = sb("a2_t", [128, 2])
    diagd = sb("diagd", [128, 512])
    xarena = sb("xarena", [128, 4096])
    x1arena = sb("x1arena", [128, 4096])
    x1lo = sb("x1lo", [128, 4096], BF16)
    big = sb("big", [128, 16384])
    oarena = sb("oarena", [128, 4096])
    kvA = sb("kvA", [128, 2048])
    kvB = sb("kvB", [128, 2048])
    vts = sb("vts", [128, 256])
    kts = sb("kts", [128, 256])
    qdec = sb("qdec", [128, 256])
    kqd = sb("kqd", [128, 512])
    sq0 = sb("sq0", [128, 256])
    sq1 = sb("sq1", [128, 256])
    stage = sq0
    stageg = sq1
    rb1 = sb("rb1", [128, 512])
    m_row = sb("m_row", [1, 256])
    m2_row = sb("m2_row", [1, 256])
    var_row = sb("var_row", [1, 256])
    sd_row = sb("sd_row", [1, 256])
    R_row = sb("R_row", [1, 256])
    MR_row = sb("MR_row", [1, 256])
    warena = sb("warena", [128, 12288], BF16)

    r3b = rb1[:, 0:256]

    pA = psum("pA")
    pB = psum("pB")
    pq = psum("pq")
    ptr = psum("ptr")
    po = psum("po")
    pkv = psum("pkv")
    pstx = psum("pstx")
    pstq = psum("pstq")

    L = Ledger(nc, es, ["cl"] + ["wl%d" % i for i in range(6)]
                       + ["wr%d" % i for i in range(6)]
                       + ["xl", "mm", "ve", "se", "stA", "stB", "gp",
                          "ppsA", "ppsB", "pplA", "pplB"])

    TEN, VEC, SCA, SYN, GPS = nc.tensor, nc.vector, nc.scalar, nc.sync, nc.gpsimd

    def hi_view(tensor, col0, ncols):
        return tensor[:].bitcast(BF16)[:, 2 * col0 + 1:2 * (col0 + ncols):2]

    def xc(c):
        return xarena[:, c * 256:(c + 1) * 256]

    def x1c(c):
        return x1arena[:, c * 256:(c + 1) * 256]

    def x1hi(c):
        return hi_view(x1arena, c * 256, 256)

    def x1loc(c):
        return x1lo[:, c * 256:(c + 1) * 256]

    def oc(c):
        return oarena[:, c * 256:(c + 1) * 256]

    def qh(h):
        return big[:, h * 256:(h + 1) * 256]

    def kh(h):
        return big[:, 4096 + h * 256:4096 + (h + 1) * 256]

    def vh(h):
        return big[:, 8192 + h * 256:8192 + (h + 1) * 256]

    def uh(h):
        return big[:, 12288 + h * 256:12288 + (h + 1) * 256]

    def x3c(c):
        return big[:, c * 256:(c + 1) * 256]

    def x3hi(c):
        return hi_view(big, c * 256, 256)

    def sgc(par):
        return big[:, 4096 + par * 256:4096 + (par + 1) * 256]

    def hdnhi(c):
        return big[:].bitcast(BF16)[:, 9216 + c * 256:9216 + (c + 1) * 256]

    def hdnlo(c):
        return big[:].bitcast(BF16)[:, 20480 + c * 256:20480 + (c + 1) * 256]

    state = {"wctr": 0, "wgoal": {}, "ready": {}, "bankw": {}, "bankr": {}, "blk": 0}
    R = state["ready"]

    def load_w(dram_ap, ncols, family, f32=False):
        i = state["wctr"]
        state["wctr"] += 1
        slot = i % 6
        if f32:
            wf = warena[:].bitcast(F32)
            dst = wf[:, slot * 1024:slot * 1024 + ncols]
        else:
            dst = warena[:, slot * 2048:slot * 2048 + ncols]
        if family is not None:
            if isinstance(family, list):
                for fk in family:
                    L.wait(SYN, "SYN", R.get(fk, R.get(("wfam", "ffn"), None)))
            else:
                L.wait(SYN, "SYN", R.get(("wfam", family), None))
        if i >= 6:
            L.wait(SYN, "SYN", ("wr%d" % slot, i // 6))
        state["wgoal"][i] = L.inc(SYN.dma_start(out=dst, in_=dram_ap),
                                  "wl%d" % slot, 16)
        return i, dst, slot * 2048

    def w_release_name(i):
        return "wr%d" % (i % 6)

    def proj_matmuls_hilo(segments, xhi_fn, xlo_fn, bank, extra_waits=(),
                          w_split=True, x_marks=None):
        for w in extra_waits:
            L.wait(TEN, "TEN", w)
        mark = None
        first = True
        last_seg = len(segments) - 1
        for si, (cid, wsl, ks, strided_hi) in enumerate(segments):
            L.wait(TEN, "TEN", state["wgoal"][cid])
            if first:
                L.wait(TEN, "TEN", state["bankr"].get(bank.name, None))
            for ki, k in enumerate(ks):
                final = (si == last_seg and ki == len(ks) - 1)
                if x_marks is not None:
                    L.wait(TEN, "TEN", x_marks[k])
                if w_split:
                    whi = wsl[:, ki * 128:(ki + 1) * 128]
                    wlo = wsl[:, 1024 + ki * 128:1024 + (ki + 1) * 128]
                    TEN.matmul(bank[:, 0:256], whi, xhi_fn(k), start=first,
                               stop=False)
                    TEN.matmul(bank[:, 0:256], whi, xlo_fn(k), start=False,
                               stop=False)
                    instr = TEN.matmul(bank[:, 0:256], wlo, xhi_fn(k), start=False,
                                       stop=final)
                else:
                    if strided_hi:
                        whi = wsl[:, 2 * ki * 128 + 1:2 * (ki + 1) * 128:2]
                    else:
                        whi = wsl[:, ki * 128:(ki + 1) * 128]
                    TEN.matmul(bank[:, 0:256], whi, xhi_fn(k), start=first,
                               stop=False)
                    instr = TEN.matmul(bank[:, 0:256], whi, xlo_fn(k), start=False,
                                       stop=final)
                first = False
                if ki == len(ks) - 1:
                    L.inc(instr, w_release_name(cid), 1)
                    if final:
                        mark = L.cur(w_release_name(cid))
        state["bankw"][bank.name] = mark
        return mark

    def bank_read_done(bank, mark):
        state["bankr"][bank.name] = [mark]

    def bank_read_add(bank, mark):
        state["bankr"].setdefault(bank.name, []).append(mark)

    with es:
        es.enter_context(nc.Block())

        for dst, src in [(ident, ident_d), (ones_col, ones_col_d),
                         (ones_row, ones_row_d), (miota, miota_d),
                         (negm1_r, negm1_d), (negk_r, negk_d), (neg256, neg256_d),
                         (a2_t, a2_d),
                         (gs_t, gs_d), (us_t, us_d), (ds_t, ds_d), (sl_t, sl_d)]:
            L.inc(GPS.dma_start(out=dst[:], in_=src[:, :]), "cl", 16)
        setup_xl = L.cur("cl")

        VEC.memset(epsln[:], EPS_LN)
        VEC.memset(epsrms[:], EPS_RMS)
        L.inc(VEC.memset(kvA[:], 0.0), "ve", 1)
        R["kv0"] = L.cur("ve")
        L.wait(VEC, "VEC", setup_xl)

        def pp_in(s):
            return oarena[:, s * 2048:s * 2048 + 1024]

        def pp_out(s):
            return oarena[:].bitcast(BF16)[:, s * 4096 + 2048:(s + 1) * 4096]

        PPC = 1024
        pp_jobs = []
        kv_order = []
        for h in range(16):
            kv_order += [16 + h, 32 + h]
        order = kv_order + list(range(16)) + list(range(48, 64))
        for fo in order:
            for pc in range(2):
                pp_jobs.append((wq_d[fo][:, pc * PPC:(pc + 1) * PPC],
                                wql_d[fo][:, pc * 2048:(pc + 1) * 2048], PPC, True,
                                ("wpp", "q", fo, pc)))
        n_kv_jobs = 64
        for fo in range(16):
            for pc in range(2):
                pp_jobs.append((wo_d[fo][:, pc * PPC:(pc + 1) * PPC],
                                wol_d[fo][:, pc * 2048:(pc + 1) * 2048], PPC, True,
                                ("wpp", "o", fo, pc)))

        for fo in range(44):
            for pc in range(2):
                pp_jobs.append((wg_d[fo][:, pc * PPC:(pc + 1) * PPC],
                                wgb_d[fo][:, pc * PPC:(pc + 1) * PPC], PPC, False,
                                ("wpp", "g", fo, pc)))
        for fo in range(44):
            for pc in range(2):
                pp_jobs.append((wu_d[fo][:, pc * PPC:(pc + 1) * PPC],
                                wub_d[fo][:, pc * PPC:(pc + 1) * PPC], PPC, False,
                                ("wpp", "u", fo, pc)))
        for ho in range(16):
            for pc in range(8):
                pp_jobs.append((wd_d[ho][:, pc * 704:(pc + 1) * 704],
                                wdb_d[ho][:, pc * 704:(pc + 1) * 704], 704, False,
                                ("wpp", "d", ho, pc)))

        pp_done = {}
        pp_state = {"next": 0}

        def pp_emit(count):
            end = min(pp_state["next"] + count, len(pp_jobs))
            for i in range(pp_state["next"], end):
                srcap, dstp, ncols, split, ckey = pp_jobs[i]
                s = i % 2
                L.wait(SYN, "SYN", pp_done.get(i - 2, None))
                ld = L.inc(SYN.dma_start(out=pp_in(s)[:, 0:ncols], in_=srcap),
                           "pplA" if s == 0 else "pplB", 16)
                hi_ap = pp_out(s)[:, 0:ncols]
                L.wait(SCA, "SCA", ld, pp_done.get(i - 2, None))
                h1 = L.inc(SCA.activation(hi_ap, pp_in(s)[:, 0:ncols], AF.Copy),
                           "se", 1)
                if split:
                    lo_ap = pp_out(s)[:, 1024:1024 + ncols]
                    L.wait(VEC, "VEC", h1, pp_done.get(i - 2, None))
                    h2 = L.inc(VEC.tensor_tensor(lo_ap, pp_in(s)[:, 0:ncols],
                                                 hi_ap, ALU.subtract), "ve", 1)
                    L.wait(GPS, "GPS", h2)
                    stm = L.inc(GPS.dma_start(out=dstp, in_=pp_out(s)[:, 0:2048]),
                                "ppsA" if s == 0 else "ppsB", 16)
                else:
                    L.wait(GPS, "GPS", h1)
                    stm = L.inc(GPS.dma_start(out=dstp, in_=pp_out(s)[:, 0:ncols]),
                                "ppsA" if s == 0 else "ppsB", 16)
                pp_done[i] = stm
                if ckey is not None:
                    R[ckey] = stm
                if i == n_kv_jobs - 1:
                    R[("wfam", "kv")] = [L.cur("ppsA"), L.cur("ppsB")]
                if i == 128 - 1:
                    R[("wfam", "qu")] = [L.cur("ppsA"), L.cur("ppsB")]
                if i == 160 - 1:
                    R[("wfam", "wo")] = [L.cur("ppsA"), L.cur("ppsB")]
                if i == len(pp_jobs) - 1:
                    R[("wfam", "ffn")] = [L.cur("ppsA"), L.cur("ppsB")]
                    R["pp_oarena_free"] = R[("wfam", "ffn")]
            pp_state["next"] = end


        L.wait(SCA, "SCA", setup_xl)
        emit_ln0 = True
        for h in range(HEADS):
            instr = SCA.activation(bd_t[:, h:h + 1], neg256[:, 0:1],
                                   AF.Exp, scale=sl_t[:, h:h + 1])
        dr_mark = L.inc(instr, "se", 1)
        for h in range(HEADS):
            instr = SCA.activation(kd_cols[:, 2 * h:2 * h + 2], a2_t[:, 0:2],
                                   AF.Exp, scale=sl_t[:, h:h + 1])
        R["kdcols"] = L.inc(instr, "se", 1)
        for h in range(HEADS):
            L.wait(SCA, "SCA", L.cur("mm"))
            se_h = L.inc(SCA.activation(rowscr[0:1, 0:256], negm1_r[0:1, :],
                                        AF.Exp, scale=sl_t[0:1, h:h + 1]), "se", 1)
            L.wait(TEN, "TEN", se_h, state["bankr"].get(pq.name, None))
            mmk = L.inc(TEN.matmul(pq[:, 0:256], ones_row[:], rowscr[0:1, 0:256],
                                   start=True, stop=True), "mm", 1)
            L.wait(VEC, "VEC", mmk)
            ev = L.inc(VEC.tensor_copy(qdbc[:, h * 256:(h + 1) * 256], pq[:, 0:256]),
                       "ve", 1)
            bank_read_done(pq, ev)
        R["qdkd"] = L.cur("ve")
        R["bd"] = dr_mark
        for h in range(HEADS):
            R[("kv_ok", 0, h)] = R["kv0"]

        def ln_stats_chunk(x_ap, c, waits=()):
            L.wait(SCA, "SCA", R.get(("sq_busy", c % 2), None), *waits)
            se = L.inc(SCA.square(sq0[:] if c % 2 == 0 else sq1[:], x_ap), "se", 1)
            L.wait(TEN, "TEN", se)
            if c == 0:
                L.wait(TEN, "TEN", state["bankr"].get(pstx.name, None),
                       state["bankr"].get(pstq.name, None))
            TEN.matmul(pstx[0:1, 0:256], ones_col[:], x_ap,
                       start=(c == 0), stop=(c == NCH - 1))
            instr = TEN.matmul(pstq[0:1, 0:256], ones_col[:],
                               sq0[:] if c % 2 == 0 else sq1[:],
                               start=(c == 0), stop=(c == NCH - 1))
            R[("sq_busy", c % 2)] = L.inc(instr, "mm", 1)

        def layernorm(x_chunk_fn, dst_fn, dstlo_fn, rb, src_waits, dst_waits=None,
                      split_apply=True):
            for c in range(NCH):
                if c == 0:
                    L.wait(SCA, "SCA", src_waits)
                    L.wait(TEN, "TEN", src_waits)
                ln_stats_chunk(x_chunk_fn(c), c)
            return ln_finish(x_chunk_fn, dst_fn, dstlo_fn, rb, dst_waits,
                             split_apply)

        def ln_finish(x_chunk_fn, dst_fn, dstlo_fn, rb, dst_waits=None,
                      split_apply=True):
            mm_stats = L.cur("mm")
            L.wait(SCA, "SCA", mm_stats)
            cp = L.inc(SCA.activation(m_row[0:1, :], pstx[0:1, 0:256], AF.Copy,
                                      scale=1.0 / DM), "se", 1)
            L.wait(SCA, "SCA", cp)
            se1 = L.inc(SCA.square(m2_row[0:1, :], m_row[0:1, :]), "se", 1)
            bank_read_done(pstx, se1)
            L.wait(VEC, "VEC", se1)
            ve1 = L.inc(VEC.scalar_tensor_tensor(var_row[0:1, :], pstq[0:1, 0:256],
                                                 1.0 / DM, m2_row[0:1, :],
                                                 ALU.mult, ALU.subtract), "ve", 1)
            bank_read_done(pstq, ve1)
            L.wait(SCA, "SCA", ve1)
            se2 = L.inc(SCA.activation(sd_row[0:1, :], var_row[0:1, :], AF.Sqrt,
                                       bias=epsln[0:1, 0:1]), "se", 1)
            L.wait(VEC, "VEC", se2)
            rv = L.inc(VEC.reciprocal(R_row[0:1, :], sd_row[0:1, :]), "ve", 1)
            L.wait(VEC, "VEC", rv)
            ve2 = L.inc(VEC.tensor_mul(MR_row[0:1, :], m_row[0:1, :], R_row[0:1, :]),
                        "ve", 1)
            L.wait(TEN, "TEN", ve2, state["bankr"].get(pq.name, None))
            TEN.matmul(pq[:, 0:256], ones_row[:], R_row[0:1, :], start=True, stop=True)
            instr = TEN.matmul(pq[:, 256:512], ones_row[:], MR_row[0:1, :],
                               start=True, stop=True)
            mmb = L.inc(instr, "mm", 1)
            L.wait(VEC, "VEC", mmb)
            ve3 = L.inc(VEC.tensor_copy(rb[:, 0:512], pq[:, 0:512]), "ve", 1)
            bank_read_done(pq, ve3)
            L.wait(VEC, "VEC", ve3)
            L.wait(GPS, "GPS", ve3)
            chunk_marks = []
            for c in range(NCH):
                use_gps = split_apply and (c % 3 == 2)
                eng, en, sem, stg2 = ((GPS, "GPS", "gp", stageg) if use_gps
                                      else (VEC, "VEC", "ve", stage))
                if dst_waits is not None:
                    L.wait(eng, en, dst_waits(c))
                m1 = L.inc(eng.tensor_tensor(stg2[:], x_chunk_fn(c), rb[:, 0:256],
                                             ALU.mult), sem, 1)
                L.wait(eng, en, m1)
                m2 = L.inc(eng.tensor_tensor(dst_fn(c), stg2[:], rb[:, 256:512],
                                             ALU.subtract), sem, 1)
                L.wait(eng, en, m2)
                dst_ap = dst_fn(c)
                m3 = L.inc(eng.tensor_tensor(dstlo_fn(c), dst_ap,
                                             dst_ap.bitcast(BF16)[:, 1::2],
                                             ALU.subtract), sem, 1)
                L.wait(eng, en, m3)
                chunk_marks.append(m3)
            return chunk_marks

        def attn_A(h, dup):
            if not dup:
                L.wait(SCA, "SCA", R.get(("diagd_free",), None))
                sed = L.inc(SCA.activation(diagd[:], miota[:], AF.Exp,
                                           scale=sl_t[:, h:h + 1]), "se", 1)
                L.wait(TEN, "TEN", state["bankr"].get(pq.name, None),
                       R[("q", h)], R[("k", h)])
                TEN.matmul(pq[:, 0:256], kh(h)[:, 0:128], qh(h), start=True, stop=True)
                mmq = L.inc(TEN.matmul(pq[:, 256:512], kh(h)[:, 128:256], qh(h),
                                       start=True, stop=True), "mm", 1)
                state["bankw"][pq.name] = mmq
                R[("mmq", h)] = mmq
            if not dup:
                L.wait(VEC, "VEC", R[("mmq", h)], sed, R.get(("kqd_free",), None))
                VEC.tensor_tensor(kqd[:, 0:256], pq[:, 0:256], diagd[:, 0:256],
                                  ALU.mult)
                vek = L.inc(VEC.tensor_tensor(kqd[:, 256:512], pq[:, 256:512],
                                              diagd[:, 256:512], ALU.mult), "ve", 1)
                bank_read_done(pq, vek)
                R[("diagd_free",)] = vek
                R[("vek", h)] = vek
                L.wait(VEC, "VEC", R[("q", h)], R["qdkd"],
                       R.get(("qdec_free",), None))
                veq = L.inc(VEC.tensor_tensor(qdec[:], qh(h),
                                              qdbc[:, h * 256:(h + 1) * 256],
                                              ALU.mult), "ve", 1)
                R[("veq", h)] = veq
                R[("q_used", h)] = [R[("mmq", h)], veq]
            L.wait(TEN, "TEN", state["bankr"].get(ptr.name, None), R[("v", h)],
                   R[("k", h)])
            TEN.transpose(ptr[:, 0:128], vh(h)[:, 0:128], ident[:])
            TEN.transpose(ptr[:, 128:256], vh(h)[:, 128:256], ident[:])
            TEN.transpose(ptr[:, 256:384], kh(h)[:, 0:128], ident[:])
            mmt = L.inc(TEN.transpose(ptr[:, 384:512], kh(h)[:, 128:256], ident[:]),
                        "mm", 1)
            R[("v_used", h)] = mmt
            L.wait(VEC, "VEC", mmt, R["kdcols"], R.get(("vts_free",), None),
                   R.get(("outbuf", 0), None), R.get(("outbuf", 1), None))
            VEC.tensor_copy(vts[:], ptr[:, 0:256])
            VEC.tensor_scalar(kts[:, 0:128], ptr[:, 256:384],
                              kd_cols[:, 2 * h:2 * h + 1], None, ALU.mult)
            vet = L.inc(VEC.tensor_scalar(kts[:, 128:256], ptr[:, 384:512],
                                          kd_cols[:, 2 * h + 1:2 * h + 2], None,
                                          ALU.mult), "ve", 1)
            bank_read_done(ptr, vet)
            R[("vet", h)] = vet
            R[("k_used", h)] = [mmt] + ([R[("mmq", h)]] if not dup else [])

        def attn_B(h, dup, first_main):
            parity = state["blk"] % 2
            kv_cur = kvA if parity == 0 else kvB
            kv_nxt = kvB if parity == 0 else kvA
            kvs_cur = kv_cur[:, h * 128:(h + 1) * 128]
            kvs_nxt = kv_nxt[:, h * 128:(h + 1) * 128]

            if not dup:
                L.wait(TEN, "TEN", state["bankr"].get(po.name, None),
                       R[("vet", h)], R[("vek", h)], R[("veq", h)],
                       R[("kv_ok", parity, h)])
                TEN.matmul(po[:, 0:256], vts[:, 0:128], kqd[:, 0:256],
                           start=True, stop=False)
                TEN.matmul(po[:, 0:256], vts[:, 128:256], kqd[:, 256:512],
                           start=False, stop=False)
                mmo = L.inc(TEN.matmul(po[:, 0:256], kvs_cur, qdec[:],
                                       start=False, stop=True), "mm", 1)
                state["bankw"][po.name] = mmo
                R[("kv_read", parity, h)] = mmo
                R[("kqd_free",)] = mmo
                R[("qdec_free",)] = mmo
            else:
                L.wait(TEN, "TEN", R[("vet", h)])
                mmo = None
            L.wait(TEN, "TEN", state["bankr"].get(pkv.name, None))
            TEN.matmul(pkv[:, 0:128], kts[:, 0:128], vts[:, 0:128],
                       start=True, stop=False)
            mmk2 = L.inc(TEN.matmul(pkv[:, 0:128], kts[:, 128:256], vts[:, 128:256],
                                    start=False, stop=True), "mm", 1)
            R[("vts_free",)] = mmk2
            if not dup and h >= 1:
                sqb_prev = sq0 if (h - 1) % 2 == 0 else sq1
                L.wait(TEN, "TEN", R[("seo", h - 1)])
                if h - 1 == 0:
                    L.wait(TEN, "TEN", state["bankr"].get(pstx.name, None))
                instr = TEN.matmul(pstx[0:1, 0:256], ones_col[:], sqb_prev[:],
                                   start=(h - 1 == 0), stop=False)
                R[("o2_free", (h - 1) % 2)] = L.inc(instr, "mm", 1)
            L.wait(VEC, "VEC", mmk2, R["bd"], R[("kv_ok", parity, h)],
                   R.get(("kv_read", parity, h), None),
                   R.get(("kv_read", 1 - parity, h), None))
            vkv = L.inc(VEC.scalar_tensor_tensor(kvs_nxt, kvs_cur, bd_t[:, h:h + 1],
                                                 pkv[:, 0:128], ALU.mult, ALU.add),
                        "ve", 1)
            bank_read_done(pkv, vkv)
            R[("kv_ok", 1 - parity, h)] = vkv

            if not dup:
                L.wait(VEC, "VEC", mmo)
                if first_main and h == 0:
                    L.wait(VEC, "VEC", R["pp_oarena_free"])
                veo = L.inc(VEC.tensor_copy(oc(h), po[:, 0:256]), "ve", 1)
                sqb = sq0 if h % 2 == 0 else sq1
                L.wait(SCA, "SCA", mmo, R.get(("o2_free", h % 2), None))
                seo = L.inc(SCA.square(sqb[:], po[:, 0:256]), "se", 1)
                R[("seo", h)] = seo
                bank_read_done(po, veo)
                bank_read_add(po, seo)

        def emit_x_load(t, dup):
            x_dram = xd_d if dup else xm_d
            L.wait(GPS, "GPS", R.get("x_free", None))
            xin = xarena[:].rearrange("p (c m) -> p c m", c=NCH)
            return L.inc(GPS.dma_start(out=xin,
                                       in_=x_dram[t].rearrange("c p m -> p c m")),
                         "xl", 16)

        def emit_tile(t, dup, first_main=False):

            pre = R.pop(("x1_ready", dup, t), None)
            if pre is not None:
                x1_ready = pre
            else:
                key = ("tile_xl", dup, t)
                tile_xl = R.pop(key, None) or emit_x_load(t, dup)
                x1_ready = layernorm(xc, x1c, x1loc, rb1, tile_xl,
                                     dst_waits=lambda c: R.get("x1_free", None))
            R["x1"] = x1_ready

            banks = [pA, pB]
            pidx = [0]

            def proj_chunk(fo):
                segs = []
                for half in range(2):
                    cid, wsl, _ = load_w(wql_d[fo][:, half * 2048:(half + 1) * 2048],
                                         2048, [("wpp", "q", fo, half)])
                    segs.append((cid, wsl, list(range(half * 8, half * 8 + 8)), False))
                bank = banks[pidx[0] % 2]
                pidx[0] += 1
                mark = proj_matmuls_hilo(segs, x1hi, x1loc, bank,
                                         x_marks=x1_ready)
                role, hh = divmod(fo, 16)
                if role == 0:
                    L.wait(SCA, "SCA", mark, R.get(("q_used", hh), None))
                    ev = L.inc(SCA.activation(qh(hh), bank[:, 0:256], AF.Silu),
                               "se", 1)
                    R[("q", hh)] = ev
                elif role == 1:
                    L.wait(SCA, "SCA", mark, R.get(("k_used", hh), None))
                    ev = L.inc(SCA.activation(kh(hh), bank[:, 0:256], AF.Silu),
                               "se", 1)
                    R[("k", hh)] = ev
                elif role == 2:
                    L.wait(SCA, "SCA", mark, R.get(("v_used", hh), None))
                    ev = L.inc(SCA.copy(vh(hh), bank[:, 0:256]), "se", 1)
                    R[("v", hh)] = ev
                else:
                    L.wait(SCA, "SCA", mark, R.get(("u_used", hh), None))
                    ev = L.inc(SCA.copy(uh(hh), bank[:, 0:256]), "se", 1)
                    R[("u", hh)] = ev
                bank_read_done(bank, ev)
                R["last_proj_mark"] = mark

            roles = (1, 2) if dup else (0, 1, 2, 3)
            for h in range(HEADS):
                for role in roles:
                    proj_chunk(role * 16 + h)
                    if dup:
                        pp_emit(2)
                if h >= 1:
                    attn_B(h - 1, dup, first_main)
                attn_A(h, dup)
            attn_B(HEADS - 1, dup, first_main)
            if not dup:
                hh = HEADS - 1
                sqb_prev = sq0 if hh % 2 == 0 else sq1
                L.wait(TEN, "TEN", R[("seo", hh)])
                instr = TEN.matmul(pstx[0:1, 0:256], ones_col[:], sqb_prev[:],
                                   start=False, stop=True)
                R[("o2_free", hh % 2)] = L.inc(instr, "mm", 1)
            state["blk"] += 1

            if dup:
                R["x_free"] = x1_ready
                R["x1_free"] = R["last_proj_mark"]
                if t + 1 < n_dup:
                    xl_next = emit_x_load(t + 1, True)
                    R[("x1_ready", True, t + 1)] = layernorm(
                        xc, x1c, x1loc, rb1, xl_next,
                        dst_waits=lambda c: R.get("x1_free", None))
                elif n_main > 0:
                    xl_next = emit_x_load(0, False)
                    R[("x1_ready", False, 0)] = layernorm(
                        xc, x1c, x1loc, rb1, xl_next,
                        dst_waits=lambda c: R.get("x1_free", None))
                return

            mm_rms = L.cur("mm")
            L.wait(SCA, "SCA", mm_rms)
            se1 = L.inc(SCA.activation(m_row[0:1, :], pstx[0:1, 0:256], AF.Copy,
                                       scale=1.0 / DM), "se", 1)
            bank_read_done(pstx, se1)
            L.wait(SCA, "SCA", se1)
            se2 = L.inc(SCA.activation(sd_row[0:1, :], m_row[0:1, :], AF.Sqrt,
                                       bias=epsrms[0:1, 0:1]), "se", 1)
            L.wait(VEC, "VEC", se2)
            ve1 = L.inc(VEC.reciprocal(R_row[0:1, :], sd_row[0:1, :]), "ve", 1)
            L.wait(TEN, "TEN", ve1, state["bankr"].get(pq.name, None))
            mmb = L.inc(TEN.matmul(pq[:, 0:256], ones_row[:], R_row[0:1, :],
                                   start=True, stop=True), "mm", 1)
            L.wait(VEC, "VEC", mmb)
            ve2 = L.inc(VEC.tensor_copy(r3b, pq[:, 0:256]), "ve", 1)
            bank_read_done(pq, ve2)

            L.wait(VEC, "VEC", R[("u", NCH - 1)])
            L.wait(VEC, "VEC", ve2)
            L.wait(GPS, "GPS", R[("u", NCH - 1)], ve2)
            x2in_marks = []
            for c in range(NCH):
                use_gps = (c % 3 == 2)
                eng, en, sem, stg2 = ((GPS, "GPS", "gp", stageg) if use_gps
                                      else (VEC, "VEC", "ve", stage))
                L.wait(eng, en, R[("u", c)])
                m1 = L.inc(eng.tensor_tensor(stg2[:], oc(c), r3b, ALU.mult),
                           sem, 1)
                L.wait(eng, en, m1)
                m2 = L.inc(eng.tensor_tensor(x1c(c), stg2[:], uh(c), ALU.mult),
                           sem, 1)
                L.wait(eng, en, m2)
                m3 = L.inc(eng.tensor_tensor(x1loc(c), x1c(c), x1hi(c),
                                             ALU.subtract), sem, 1)
                L.wait(eng, en, m3)
                x2in_marks.append(m3)
            x2in_ready = [x2in_marks[NCH - 1], x2in_marks[NCH - 2]]
            for c in range(NCH):
                R[("u_used", c)] = x2in_ready

            x_free_marks = []
            for fo in range(NCH):
                segs = []
                for half in range(2):
                    cid, wsl, _ = load_w(wol_d[fo][:, half * 2048:(half + 1) * 2048],
                                         2048, [("wpp", "o", fo, half)])
                    segs.append((cid, wsl, list(range(half * 8, half * 8 + 8)), False))
                bank = banks[fo % 2]
                mark = proj_matmuls_hilo(segs, x1hi, x1loc, bank,
                                         x_marks=x2in_marks)
                L.wait(VEC, "VEC", mark)
                ev = L.inc(VEC.tensor_tensor(oc(fo), bank[:, 0:256], xc(fo),
                                             ALU.add), "ve", 1)
                bank_read_done(bank, ev)
                x_free_marks.append(ev)
            R["x_free"] = x_free_marks[-1]
            R["x2"] = x_free_marks
            R["x1_free"] = x_free_marks[-1]

            x3_ready = layernorm(oc, x3c, x1loc, rb1, R["x2"],
                                 dst_waits=lambda c: [R.get(("q_used", c), None),
                                                      R["x1_free"]])

            if t + 1 < n_main:
                R[("tile_xl", False, t + 1)] = emit_x_load(t + 1, False)

            gbanks = [pA, ptr]
            ubanks = [pB, po]
            for fo in range(ICH):
                par = fo % 2
                gbank = gbanks[par]
                ubank = ubanks[par]
                cidg, wslg, _ = load_w(wgb_d[fo], 2048, [("wpp", "g", fo, 0), ("wpp", "g", fo, 1)])
                markg = proj_matmuls_hilo([(cidg, wslg, list(range(NCH)), False)],
                                          x3hi, x1loc, gbank,
                                          w_split=False, x_marks=x3_ready)
                L.wait(SCA, "SCA", markg, R.get(("sg_used", par), None))
                seg = L.inc(SCA.activation(sgc(par), gbank[:, 0:256], AF.Silu,
                                           scale=gs_t[:, fo:fo + 1]), "se", 1)
                bank_read_done(gbank, seg)
                cidu, wslu, _ = load_w(wub_d[fo], 2048, [("wpp", "u", fo, 0), ("wpp", "u", fo, 1)])
                marku = proj_matmuls_hilo([(cidu, wslu, list(range(NCH)), False)],
                                          x3hi, x1loc, ubank,
                                          w_split=False, x_marks=x3_ready)
                L.wait(VEC, "VEC", marku, seg)
                n1 = L.inc(VEC.scalar_tensor_tensor(stage[:], ubank[:, 0:256],
                                                    us_t[:, fo:fo + 1], sgc(par),
                                                    ALU.mult, ALU.mult), "ve", 1)
                bank_read_done(ubank, n1)
                L.wait(VEC, "VEC", n1)
                n2 = L.inc(VEC.tensor_copy(hdnhi(fo), stage[:]), "ve", 1)
                L.wait(VEC, "VEC", n2)
                n3 = L.inc(VEC.tensor_tensor(hdnlo(fo), stage[:], hdnhi(fo),
                                             ALU.subtract), "ve", 1)
                L.wait(VEC, "VEC", n3)
                R[("sg_used", par)] = n1
            hdn_ready = L.cur("ve")

            if t + 1 < n_main:
                xl_next = R.pop(("tile_xl", False, t + 1))
                R[("x1_ready", False, t + 1)] = layernorm(
                    xc, x1c, x1loc, rb1, [xl_next, hdn_ready],
                    dst_waits=lambda c: R.get("x1_free", None))

            dbanks = [pA, pB, ptr, po]
            for ho in range(NCH):
                bank = dbanks[ho % 4]
                for quarter in range(4):
                    cid, wsl, _ = load_w(
                        wdb_d[ho][:, quarter * 1408:(quarter + 1) * 1408],
                        1408, [("wpp", "d", ho, 2 * quarter),
                               ("wpp", "d", ho, 2 * quarter + 1)])
                    L.wait(TEN, "TEN", state["wgoal"][cid])
                    if quarter == 0:
                        L.wait(TEN, "TEN", state["bankr"].get(bank.name, None),
                               hdn_ready)
                    for k in range(11):
                        kk = quarter * 11 + k
                        TEN.matmul(bank[:, 0:256], wsl[:, k * 128:(k + 1) * 128],
                                   hdnhi(kk), start=(kk == 0), stop=False)
                        instr = TEN.matmul(bank[:, 0:256],
                                           wsl[:, k * 128:(k + 1) * 128],
                                           hdnlo(kk), start=False,
                                           stop=(kk == ICH - 1 and k == 10))
                    mark = L.inc(instr, w_release_name(cid), 1)
                state["bankw"][bank.name] = mark
                par = ho % 2
                obuf = vts if par == 0 else kts
                L.wait(VEC, "VEC", mark, R.get(("outbuf", par), None))
                ev = L.inc(VEC.scalar_tensor_tensor(obuf[:],
                                                    bank[:, 0:256], ds_t[:, ho:ho + 1],
                                                    oc(ho), ALU.mult, ALU.add),
                           "ve", 1)
                bank_read_done(bank, ev)
                L.wait(GPS, "GPS", ev)
                stm = L.inc(GPS.dma_start(out=out_d[t][ho], in_=obuf[:]),
                            "stA" if par == 0 else "stB", 16)
                R[("outbuf", par)] = stm

        if n_dup > 0:
            xl0 = emit_x_load(0, True)
            R[("x1_ready", True, 0)] = layernorm(
                xc, x1c, x1loc, rb1, xl0, dst_waits=None)
        pp_emit(n_kv_jobs)

        for t in range(n_dup):
            emit_tile(t, dup=True)
        pp_emit(len(pp_jobs))
        for t in range(n_main):
            emit_tile(t, dup=False, first_main=(t == 0))

        GPS.wait_ge(L.h["stA"], L.c["stA"])
        GPS.wait_ge(L.h["stB"], L.c["stB"])

    return nc


def _prep_inputs(L0_attn_x, L0_attn_slope_rate, L0_attn_W_qkvu, L0_attn_W_out,
                 L0_ffn_gate_w, L0_ffn_up_w, L0_ffn_down_w, L0_ffn_gate_s,
                 L0_ffn_up_s, L0_ffn_down_s):
    f32 = np.float32
    x = np.ascontiguousarray(L0_attn_x, dtype=f32)
    wq = np.ascontiguousarray(
        np.asarray(L0_attn_W_qkvu, f32).reshape(64, 128, 16, 128)
        .transpose(0, 3, 2, 1).reshape(64, 128, 2048))
    wo = np.ascontiguousarray(
        np.asarray(L0_attn_W_out, f32).reshape(16, 128, 16, 128)
        .transpose(0, 3, 2, 1).reshape(16, 128, 2048))
    wg = np.ascontiguousarray(
        np.asarray(L0_ffn_gate_w, f32).reshape(44, 128, 16, 128)
        .transpose(0, 3, 2, 1).reshape(44, 128, 2048))
    wu = np.ascontiguousarray(
        np.asarray(L0_ffn_up_w, f32).reshape(44, 128, 16, 128)
        .transpose(0, 3, 2, 1).reshape(44, 128, 2048))
    wd = np.ascontiguousarray(
        np.asarray(L0_ffn_down_w, f32).reshape(16, 128, 44, 128)
        .transpose(0, 3, 2, 1).reshape(16, 128, 5632))
    gs = np.ascontiguousarray(np.asarray(L0_ffn_gate_s, f32).reshape(44, 128).T)
    us = np.ascontiguousarray(np.asarray(L0_ffn_up_s, f32).reshape(44, 128).T)
    ds = np.ascontiguousarray(np.asarray(L0_ffn_down_s, f32).reshape(16, 128).T)
    sl = np.ascontiguousarray(
        np.broadcast_to(np.asarray(L0_attn_slope_rate, f32).reshape(1, 16),
                        (128, 16)))

    def tile_x(seq_slice):
        T = seq_slice.shape[0]
        a = np.ascontiguousarray(seq_slice.T)
        return np.ascontiguousarray(
            a.reshape(16, 128, T // 256, 256).transpose(2, 0, 1, 3))

    zeros_xd = np.zeros((8, 16, 128, 256), f32)
    in_maps = []
    for core in range(8):
        p, half = core // 2, core % 2
        xm = tile_x(x[p, half * 2048:(half + 1) * 2048])
        xd = tile_x(x[p, 0:2048]) if half == 1 else zeros_xd
        in_maps.append({
            "xd": xd, "xm": xm, "wq": wq, "wo": wo, "wg": wg, "wu": wu,
            "wd": wd, "gs": gs, "us": us, "ds": ds, "sl": sl,
        })
    return in_maps


def kernel(**inputs):
    in_maps = _prep_inputs(**inputs)
    nc = build_program(8, 8)
    res = run_bass_kernel_spmd(nc, in_maps, core_ids=list(range(8)))
    out = np.empty((4, 4096, 2048), np.float32)
    for core in range(8):
        p, half = core // 2, core % 2
        o = np.asarray(res.results[core]["out"])
        o = o.transpose(1, 2, 0, 3).reshape(2048, 2048)
        out[p, half * 2048:(half + 1) * 2048, :] = o.T
    return out


# version 22
# speedup vs baseline: 1.1988x; 1.0491x over previous
import contextlib
import numpy as np

import concourse.bass as bass
import concourse.mybir as mybir
from concourse.bass_utils import run_bass_kernel_spmd

F32 = mybir.dt.float32
BF16 = mybir.dt.bfloat16
AF = mybir.ActivationFunctionType
ALU = mybir.AluOpType

DM = 2048
NCH = 16
HEADS = 16
BLK = 256
INTER = 5632
ICH = 44
EPS_LN = 1e-5
EPS_RMS = 1e-6
MASK_NEG = -1.0e5


class Ledger:
    def __init__(self, nc, es, names):
        self.nc = nc
        self.h = {}
        self.c = {}
        self.lw = {}
        for n in names:
            self.h[n] = es.enter_context(nc.semaphore(n))
            self.c[n] = 0

    def inc(self, instr, name, k):
        instr.then_inc(self.h[name], k)
        self.c[name] += k
        return (name, self.c[name])

    def wait(self, eng, engname, *marks):
        for m in marks:
            if m is None:
                continue
            if isinstance(m, list):
                self.wait(eng, engname, *m)
                continue
            name, val = m
            if val <= 0:
                continue
            key = (engname, name)
            if self.lw.get(key, 0) >= val:
                continue
            eng.wait_ge(self.h[name], val)
            self.lw[key] = val

    def cur(self, name):
        return (name, self.c[name])


def build_program(n_dup=8, n_main=8):
    nc = bass.Bass("TRN2", target_bir_lowering=False, debug=False)

    xd_d = nc.dram_tensor("xd", [n_dup, NCH, 128, BLK], F32, kind="ExternalInput").ap()
    xm_d = nc.dram_tensor("xm", [n_main, NCH, 128, BLK], F32, kind="ExternalInput").ap()
    wq_d = nc.dram_tensor("wq", [64, 128, 2048], F32, kind="ExternalInput").ap()
    wo_d = nc.dram_tensor("wo", [16, 128, 2048], F32, kind="ExternalInput").ap()
    wg_d = nc.dram_tensor("wg", [44, 128, 2048], F32, kind="ExternalInput").ap()
    wu_d = nc.dram_tensor("wu", [44, 128, 2048], F32, kind="ExternalInput").ap()
    wd_d = nc.dram_tensor("wd", [16, 128, 5632], F32, kind="ExternalInput").ap()
    gs_d = nc.dram_tensor("gs", [128, 44], F32, kind="ExternalInput").ap()
    us_d = nc.dram_tensor("us", [128, 44], F32, kind="ExternalInput").ap()
    ds_d = nc.dram_tensor("ds", [128, 16], F32, kind="ExternalInput").ap()
    sl_d = nc.dram_tensor("sl", [128, 16], F32, kind="ExternalInput").ap()
    out_d = nc.dram_tensor("out", [n_main, NCH, 128, BLK], F32, kind="ExternalOutput").ap()

    wql_d = nc.dram_tensor("wql", [64, 128, 4096], BF16, kind="Internal").ap()
    wol_d = nc.dram_tensor("wol", [16, 128, 4096], BF16, kind="Internal").ap()
    wgb_d = nc.dram_tensor("wgb", [44, 128, 2048], BF16, kind="Internal").ap()
    wub_d = nc.dram_tensor("wub", [44, 128, 2048], BF16, kind="Internal").ap()
    wdb_d = nc.dram_tensor("wdb", [16, 128, 5632], BF16, kind="Internal").ap()

    import ml_dtypes
    ident_d = nc.inline_tensor(np.eye(128, dtype=np.float32), name="c_ident").ap()
    ones_col_d = nc.inline_tensor(np.ones((128, 1), np.float32), name="c_onesc").ap()
    ones_row_d = nc.inline_tensor(np.ones((1, 128), np.float32), name="c_onesr").ap()
    mi = np.full((128, 512), MASK_NEG, np.float32)
    for j in range(2):
        s = (np.arange(128) + j * 128)[:, None]
        m = np.arange(256)[None, :]
        v = (s - m).astype(np.float32)
        mi[:, j * 256:(j + 1) * 256] = np.where(m >= s, v, MASK_NEG)
    miota_d = nc.inline_tensor(mi.astype(ml_dtypes.bfloat16), name="c_miota").ap()
    negm1 = (-(np.arange(256, dtype=np.float32) + 1.0)).reshape(1, 256)
    negk = (np.arange(256, dtype=np.float32) - 255.0).reshape(1, 256)
    negm1_d = nc.inline_tensor(negm1.astype(ml_dtypes.bfloat16), name="c_negm1").ap()
    negk_d = nc.inline_tensor(negk.astype(ml_dtypes.bfloat16), name="c_negk").ap()
    neg256_d = nc.inline_tensor(np.full((128, 1), -256.0, np.float32), name="c_n256").ap()
    a2 = np.stack([np.arange(128) - 255.0, np.arange(128) - 127.0],
                  axis=1).astype(np.float32)
    a2_d = nc.inline_tensor(a2, name="c_a2").ap()

    es = contextlib.ExitStack()
    sb = lambda name, shape, dt=F32: es.enter_context(nc.sbuf_tensor(name, shape, dt))
    psum = lambda name: es.enter_context(nc.psum_tensor(name, [128, 512], F32))

    ident = sb("ident", [128, 128])
    ones_col = sb("ones_col", [128, 1])
    ones_row = sb("ones_row", [1, 128])
    miota = sb("miota", [128, 512], BF16)
    negm1_r = sb("negm1_r", [1, 256], BF16)
    negk_r = sb("negk_r", [1, 256], BF16)
    neg256 = sb("neg256", [128, 1])
    epsln = sb("epsln", [1, 1])
    epsrms = sb("epsrms", [1, 1])
    gs_t = sb("gs_t", [128, 44])
    us_t = sb("us_t", [128, 44])
    ds_t = sb("ds_t", [128, 16])
    sl_t = sb("sl_t", [128, 16])
    bd_t = sb("bd_t", [128, 16])
    rowscr = sb("rowscr", [1, 512])
    qdbc = sb("qdbc", [128, 4096])
    kd_cols = sb("kd_cols", [128, 32])
    a2_t = sb("a2_t", [128, 2])
    diagd = sb("diagd", [128, 512])
    xarena = sb("xarena", [128, 4096])
    x1arena = sb("x1arena", [128, 4096])
    x1lo = sb("x1lo", [128, 4096], BF16)
    big = sb("big", [128, 16384])
    oarena = sb("oarena", [128, 4096])
    kvA = sb("kvA", [128, 2048])
    kvB = sb("kvB", [128, 2048])
    vts = sb("vts", [128, 256])
    kts = sb("kts", [128, 256])
    qdec = sb("qdec", [128, 256])
    kqd = sb("kqd", [128, 512])
    sq0 = sb("sq0", [128, 256])
    sq1 = sb("sq1", [128, 256])
    stage = sq0
    stageg = sq1
    rb1 = sb("rb1", [128, 512])
    m_row = sb("m_row", [1, 256])
    m2_row = sb("m2_row", [1, 256])
    var_row = sb("var_row", [1, 256])
    sd_row = sb("sd_row", [1, 256])
    R_row = sb("R_row", [1, 256])
    MR_row = sb("MR_row", [1, 256])
    warena = sb("warena", [128, 12288], BF16)

    r3b = rb1[:, 0:256]

    pA = psum("pA")
    pB = psum("pB")
    pq = psum("pq")
    ptr = psum("ptr")
    po = psum("po")
    pkv = psum("pkv")
    pstx = psum("pstx")
    pstq = psum("pstq")

    L = Ledger(nc, es, ["cl"] + ["wl%d" % i for i in range(6)]
                       + ["wr%d" % i for i in range(6)]
                       + ["xl", "mm", "ve", "se", "stA", "stB", "gp",
                          "ppsA", "ppsB", "pplA", "pplB"])

    TEN, VEC, SCA, SYN, GPS = nc.tensor, nc.vector, nc.scalar, nc.sync, nc.gpsimd

    def hi_view(tensor, col0, ncols):
        return tensor[:].bitcast(BF16)[:, 2 * col0 + 1:2 * (col0 + ncols):2]

    def xc(c):
        return xarena[:, c * 256:(c + 1) * 256]

    def x1c(c):
        return x1arena[:, c * 256:(c + 1) * 256]

    def x1hi(c):
        return hi_view(x1arena, c * 256, 256)

    def x1loc(c):
        return x1lo[:, c * 256:(c + 1) * 256]

    def oc(c):
        return oarena[:, c * 256:(c + 1) * 256]

    def qh(h):
        return big[:, h * 256:(h + 1) * 256]

    def kh(h):
        return big[:, 4096 + h * 256:4096 + (h + 1) * 256]

    def vh(h):
        return big[:, 8192 + h * 256:8192 + (h + 1) * 256]

    def uh(h):
        return big[:, 12288 + h * 256:12288 + (h + 1) * 256]

    def x3c(c):
        return big[:, c * 256:(c + 1) * 256]

    def x3hi(c):
        return hi_view(big, c * 256, 256)

    def sgc(par):
        return big[:, 4096 + par * 256:4096 + (par + 1) * 256]

    def hdnhi(c):
        return big[:].bitcast(BF16)[:, 9216 + c * 256:9216 + (c + 1) * 256]

    def hdnlo(c):
        return big[:].bitcast(BF16)[:, 20480 + c * 256:20480 + (c + 1) * 256]

    state = {"wctr": 0, "wgoal": {}, "ready": {}, "bankw": {}, "bankr": {}, "blk": 0}
    R = state["ready"]

    def load_w(dram_ap, ncols, family, f32=False):
        i = state["wctr"]
        state["wctr"] += 1
        slot = i % 6
        if f32:
            wf = warena[:].bitcast(F32)
            dst = wf[:, slot * 1024:slot * 1024 + ncols]
        else:
            dst = warena[:, slot * 2048:slot * 2048 + ncols]
        if family is not None:
            if isinstance(family, list):
                for fk in family:
                    L.wait(SYN, "SYN", R.get(fk, R.get(("wfam", "ffn"), None)))
            else:
                L.wait(SYN, "SYN", R.get(("wfam", family), None))
        if i >= 6:
            L.wait(SYN, "SYN", ("wr%d" % slot, i // 6))
        state["wgoal"][i] = L.inc(SYN.dma_start(out=dst, in_=dram_ap),
                                  "wl%d" % slot, 16)
        return i, dst, slot * 2048

    def w_release_name(i):
        return "wr%d" % (i % 6)

    def proj_matmuls_hilo(segments, xhi_fn, xlo_fn, bank, extra_waits=(),
                          w_split=True, x_marks=None):
        for w in extra_waits:
            L.wait(TEN, "TEN", w)
        mark = None
        first = True
        last_seg = len(segments) - 1
        for si, (cid, wsl, ks, strided_hi) in enumerate(segments):
            L.wait(TEN, "TEN", state["wgoal"][cid])
            if first:
                L.wait(TEN, "TEN", state["bankr"].get(bank.name, None))
            for ki, k in enumerate(ks):
                final = (si == last_seg and ki == len(ks) - 1)
                if x_marks is not None:
                    L.wait(TEN, "TEN", x_marks[k])
                if w_split:
                    whi = wsl[:, ki * 128:(ki + 1) * 128]
                    wlo = wsl[:, 1024 + ki * 128:1024 + (ki + 1) * 128]
                    TEN.matmul(bank[:, 0:256], whi, xhi_fn(k), start=first,
                               stop=False)
                    TEN.matmul(bank[:, 0:256], whi, xlo_fn(k), start=False,
                               stop=False)
                    instr = TEN.matmul(bank[:, 0:256], wlo, xhi_fn(k), start=False,
                                       stop=final)
                else:
                    if strided_hi:
                        whi = wsl[:, 2 * ki * 128 + 1:2 * (ki + 1) * 128:2]
                    else:
                        whi = wsl[:, ki * 128:(ki + 1) * 128]
                    TEN.matmul(bank[:, 0:256], whi, xhi_fn(k), start=first,
                               stop=False)
                    instr = TEN.matmul(bank[:, 0:256], whi, xlo_fn(k), start=False,
                                       stop=final)
                first = False
                if ki == len(ks) - 1:
                    L.inc(instr, w_release_name(cid), 1)
                    if final:
                        mark = L.cur(w_release_name(cid))
        state["bankw"][bank.name] = mark
        return mark

    def bank_read_done(bank, mark):
        state["bankr"][bank.name] = [mark]

    def bank_read_add(bank, mark):
        state["bankr"].setdefault(bank.name, []).append(mark)

    with es:
        es.enter_context(nc.Block())

        for dst, src in [(ident, ident_d), (ones_col, ones_col_d),
                         (ones_row, ones_row_d), (miota, miota_d),
                         (negm1_r, negm1_d), (negk_r, negk_d), (neg256, neg256_d),
                         (a2_t, a2_d),
                         (gs_t, gs_d), (us_t, us_d), (ds_t, ds_d), (sl_t, sl_d)]:
            L.inc(GPS.dma_start(out=dst[:], in_=src[:, :]), "cl", 16)
        setup_xl = L.cur("cl")

        VEC.memset(epsln[:], EPS_LN)
        VEC.memset(epsrms[:], EPS_RMS)
        L.inc(VEC.memset(kvA[:], 0.0), "ve", 1)
        R["kv0"] = L.cur("ve")
        L.wait(VEC, "VEC", setup_xl)

        def pp_in(s):
            return oarena[:, s * 2048:s * 2048 + 1024]

        def pp_out(s):
            return oarena[:].bitcast(BF16)[:, s * 4096 + 2048:(s + 1) * 4096]

        PPC = 1024
        pp_jobs = []
        kv_order = []
        for h in range(16):
            kv_order += [16 + h, 32 + h]
        order = kv_order + list(range(16)) + list(range(48, 64))
        for fo in order:
            for pc in range(2):
                pp_jobs.append((wq_d[fo][:, pc * PPC:(pc + 1) * PPC],
                                wql_d[fo][:, pc * 2048:(pc + 1) * 2048], PPC, True,
                                ("wpp", "q", fo, pc)))
        n_kv_jobs = 64
        for fo in range(16):
            for pc in range(2):
                pp_jobs.append((wo_d[fo][:, pc * PPC:(pc + 1) * PPC],
                                wol_d[fo][:, pc * 2048:(pc + 1) * 2048], PPC, True,
                                ("wpp", "o", fo, pc)))

        for fo in range(44):
            for pc in range(2):
                pp_jobs.append((wg_d[fo][:, pc * PPC:(pc + 1) * PPC],
                                wgb_d[fo][:, pc * PPC:(pc + 1) * PPC], PPC, False,
                                ("wpp", "g", fo, pc)))
        for fo in range(44):
            for pc in range(2):
                pp_jobs.append((wu_d[fo][:, pc * PPC:(pc + 1) * PPC],
                                wub_d[fo][:, pc * PPC:(pc + 1) * PPC], PPC, False,
                                ("wpp", "u", fo, pc)))
        for ho in range(16):
            for pc in range(8):
                pp_jobs.append((wd_d[ho][:, pc * 704:(pc + 1) * 704],
                                wdb_d[ho][:, pc * 704:(pc + 1) * 704], 704, False,
                                ("wpp", "d", ho, pc)))

        pp_done = {}
        pp_state = {"next": 0}

        def pp_emit(count):
            end = min(pp_state["next"] + count, len(pp_jobs))
            for i in range(pp_state["next"], end):
                srcap, dstp, ncols, split, ckey = pp_jobs[i]
                s = i % 2
                L.wait(SYN, "SYN", pp_done.get(i - 2, None))
                ld = L.inc(SYN.dma_start(out=pp_in(s)[:, 0:ncols], in_=srcap),
                           "pplA" if s == 0 else "pplB", 16)
                hi_ap = pp_out(s)[:, 0:ncols]
                L.wait(SCA, "SCA", ld, pp_done.get(i - 2, None))
                h1 = L.inc(SCA.activation(hi_ap, pp_in(s)[:, 0:ncols], AF.Copy),
                           "se", 1)
                if split:
                    lo_ap = pp_out(s)[:, 1024:1024 + ncols]
                    L.wait(VEC, "VEC", h1, pp_done.get(i - 2, None))
                    h2 = L.inc(VEC.tensor_tensor(lo_ap, pp_in(s)[:, 0:ncols],
                                                 hi_ap, ALU.subtract), "ve", 1)
                    L.wait(GPS, "GPS", h2)
                    stm = L.inc(GPS.dma_start(out=dstp, in_=pp_out(s)[:, 0:2048]),
                                "ppsA" if s == 0 else "ppsB", 16)
                else:
                    L.wait(GPS, "GPS", h1)
                    stm = L.inc(GPS.dma_start(out=dstp, in_=pp_out(s)[:, 0:ncols]),
                                "ppsA" if s == 0 else "ppsB", 16)
                pp_done[i] = stm
                if ckey is not None:
                    R[ckey] = stm
                if i == n_kv_jobs - 1:
                    R[("wfam", "kv")] = [L.cur("ppsA"), L.cur("ppsB")]
                if i == 128 - 1:
                    R[("wfam", "qu")] = [L.cur("ppsA"), L.cur("ppsB")]
                if i == 160 - 1:
                    R[("wfam", "wo")] = [L.cur("ppsA"), L.cur("ppsB")]
                if i == len(pp_jobs) - 1:
                    R[("wfam", "ffn")] = [L.cur("ppsA"), L.cur("ppsB")]
                    R["pp_oarena_free"] = R[("wfam", "ffn")]
            pp_state["next"] = end


        L.wait(SCA, "SCA", setup_xl)
        emit_ln0 = True
        for h in range(HEADS):
            instr = SCA.activation(bd_t[:, h:h + 1], neg256[:, 0:1],
                                   AF.Exp, scale=sl_t[:, h:h + 1])
        dr_mark = L.inc(instr, "se", 1)
        for h in range(HEADS):
            instr = SCA.activation(kd_cols[:, 2 * h:2 * h + 2], a2_t[:, 0:2],
                                   AF.Exp, scale=sl_t[:, h:h + 1])
        R["kdcols"] = L.inc(instr, "se", 1)
        for h in range(HEADS):
            L.wait(SCA, "SCA", L.cur("mm"))
            se_h = L.inc(SCA.activation(rowscr[0:1, 0:256], negm1_r[0:1, :],
                                        AF.Exp, scale=sl_t[0:1, h:h + 1]), "se", 1)
            L.wait(TEN, "TEN", se_h, state["bankr"].get(pq.name, None))
            mmk = L.inc(TEN.matmul(pq[:, 0:256], ones_row[:], rowscr[0:1, 0:256],
                                   start=True, stop=True), "mm", 1)
            L.wait(VEC, "VEC", mmk)
            ev = L.inc(VEC.tensor_copy(qdbc[:, h * 256:(h + 1) * 256], pq[:, 0:256]),
                       "ve", 1)
            bank_read_done(pq, ev)
        R["qdkd"] = L.cur("ve")
        R["bd"] = dr_mark
        for h in range(HEADS):
            R[("kv_ok", 0, h)] = R["kv0"]

        def ln_stats_chunk(x_ap, c, waits=()):
            L.wait(SCA, "SCA", R.get(("sq_busy", c % 2), None), *waits)
            se = L.inc(SCA.square(sq0[:] if c % 2 == 0 else sq1[:], x_ap), "se", 1)
            L.wait(TEN, "TEN", se)
            if c == 0:
                L.wait(TEN, "TEN", state["bankr"].get(pstx.name, None),
                       state["bankr"].get(pstq.name, None))
            TEN.matmul(pstx[0:1, 0:256], ones_col[:], x_ap,
                       start=(c == 0), stop=(c == NCH - 1))
            instr = TEN.matmul(pstq[0:1, 0:256], ones_col[:],
                               sq0[:] if c % 2 == 0 else sq1[:],
                               start=(c == 0), stop=(c == NCH - 1))
            R[("sq_busy", c % 2)] = L.inc(instr, "mm", 1)

        def layernorm(x_chunk_fn, dst_fn, dstlo_fn, rb, src_waits, dst_waits=None,
                      split_apply=True):
            for c in range(NCH):
                if c == 0:
                    L.wait(SCA, "SCA", src_waits)
                    L.wait(TEN, "TEN", src_waits)
                ln_stats_chunk(x_chunk_fn(c), c)
            return ln_finish(x_chunk_fn, dst_fn, dstlo_fn, rb, dst_waits,
                             split_apply)

        def ln_finish(x_chunk_fn, dst_fn, dstlo_fn, rb, dst_waits=None,
                      split_apply=True):
            mm_stats = L.cur("mm")
            L.wait(SCA, "SCA", mm_stats)
            cp = L.inc(SCA.activation(m_row[0:1, :], pstx[0:1, 0:256], AF.Copy,
                                      scale=1.0 / DM), "se", 1)
            L.wait(SCA, "SCA", cp)
            se1 = L.inc(SCA.square(m2_row[0:1, :], m_row[0:1, :]), "se", 1)
            bank_read_done(pstx, se1)
            L.wait(VEC, "VEC", se1)
            ve1 = L.inc(VEC.scalar_tensor_tensor(var_row[0:1, :], pstq[0:1, 0:256],
                                                 1.0 / DM, m2_row[0:1, :],
                                                 ALU.mult, ALU.subtract), "ve", 1)
            bank_read_done(pstq, ve1)
            L.wait(SCA, "SCA", ve1)
            se2 = L.inc(SCA.activation(sd_row[0:1, :], var_row[0:1, :], AF.Sqrt,
                                       bias=epsln[0:1, 0:1]), "se", 1)
            L.wait(VEC, "VEC", se2)
            rv = L.inc(VEC.reciprocal(R_row[0:1, :], sd_row[0:1, :]), "ve", 1)
            L.wait(VEC, "VEC", rv)
            ve2 = L.inc(VEC.tensor_mul(MR_row[0:1, :], m_row[0:1, :], R_row[0:1, :]),
                        "ve", 1)
            L.wait(TEN, "TEN", ve2, state["bankr"].get(pq.name, None))
            TEN.matmul(pq[:, 0:256], ones_row[:], R_row[0:1, :], start=True, stop=True)
            instr = TEN.matmul(pq[:, 256:512], ones_row[:], MR_row[0:1, :],
                               start=True, stop=True)
            mmb = L.inc(instr, "mm", 1)
            L.wait(VEC, "VEC", mmb)
            ve3 = L.inc(VEC.tensor_copy(rb[:, 0:512], pq[:, 0:512]), "ve", 1)
            bank_read_done(pq, ve3)
            L.wait(VEC, "VEC", ve3)
            L.wait(GPS, "GPS", ve3)
            chunk_marks = []
            for c in range(NCH):
                use_gps = split_apply and (c % 3 == 2)
                eng, en, sem, stg2 = ((GPS, "GPS", "gp", stageg) if use_gps
                                      else (VEC, "VEC", "ve", stage))
                if dst_waits is not None:
                    L.wait(eng, en, dst_waits(c))
                m1 = L.inc(eng.tensor_tensor(stg2[:], x_chunk_fn(c), rb[:, 0:256],
                                             ALU.mult), sem, 1)
                L.wait(eng, en, m1)
                m2 = L.inc(eng.tensor_tensor(dst_fn(c), stg2[:], rb[:, 256:512],
                                             ALU.subtract), sem, 1)
                L.wait(eng, en, m2)
                dst_ap = dst_fn(c)
                m3 = L.inc(eng.tensor_tensor(dstlo_fn(c), dst_ap,
                                             dst_ap.bitcast(BF16)[:, 1::2],
                                             ALU.subtract), sem, 1)
                L.wait(eng, en, m3)
                chunk_marks.append(m3)
            return chunk_marks

        def attn_A(h, dup):
            if not dup:
                L.wait(SCA, "SCA", R.get(("diagd_free",), None))
                sed = L.inc(SCA.activation(diagd[:], miota[:], AF.Exp,
                                           scale=sl_t[:, h:h + 1]), "se", 1)
                L.wait(TEN, "TEN", state["bankr"].get(pq.name, None),
                       R[("q", h)], R[("k", h)])
                TEN.matmul(pq[:, 0:256], kh(h)[:, 0:128], qh(h), start=True, stop=True)
                mmq = L.inc(TEN.matmul(pq[:, 256:512], kh(h)[:, 128:256], qh(h),
                                       start=True, stop=True), "mm", 1)
                state["bankw"][pq.name] = mmq
                R[("mmq", h)] = mmq
            if not dup:
                L.wait(VEC, "VEC", R[("mmq", h)], sed, R.get(("kqd_free",), None))
                VEC.tensor_tensor(kqd[:, 0:256], pq[:, 0:256], diagd[:, 0:256],
                                  ALU.mult)
                vek = L.inc(VEC.tensor_tensor(kqd[:, 256:512], pq[:, 256:512],
                                              diagd[:, 256:512], ALU.mult), "ve", 1)
                bank_read_done(pq, vek)
                R[("diagd_free",)] = vek
                R[("vek", h)] = vek
                L.wait(VEC, "VEC", R[("q", h)], R["qdkd"],
                       R.get(("qdec_free",), None))
                veq = L.inc(VEC.tensor_tensor(qdec[:], qh(h),
                                              qdbc[:, h * 256:(h + 1) * 256],
                                              ALU.mult), "ve", 1)
                R[("veq", h)] = veq
                R[("q_used", h)] = [R[("mmq", h)], veq]
            L.wait(TEN, "TEN", state["bankr"].get(ptr.name, None), R[("v", h)],
                   R[("k", h)])
            TEN.transpose(ptr[:, 0:128], vh(h)[:, 0:128], ident[:])
            TEN.transpose(ptr[:, 128:256], vh(h)[:, 128:256], ident[:])
            TEN.transpose(ptr[:, 256:384], kh(h)[:, 0:128], ident[:])
            mmt = L.inc(TEN.transpose(ptr[:, 384:512], kh(h)[:, 128:256], ident[:]),
                        "mm", 1)
            R[("v_used", h)] = mmt
            L.wait(VEC, "VEC", mmt, R["kdcols"], R.get(("vts_free",), None),
                   R.get(("outbuf", 0), None), R.get(("outbuf", 1), None))
            VEC.tensor_copy(vts[:], ptr[:, 0:256])
            VEC.tensor_scalar(kts[:, 0:128], ptr[:, 256:384],
                              kd_cols[:, 2 * h:2 * h + 1], None, ALU.mult)
            vet = L.inc(VEC.tensor_scalar(kts[:, 128:256], ptr[:, 384:512],
                                          kd_cols[:, 2 * h + 1:2 * h + 2], None,
                                          ALU.mult), "ve", 1)
            bank_read_done(ptr, vet)
            R[("vet", h)] = vet
            R[("k_used", h)] = [mmt] + ([R[("mmq", h)]] if not dup else [])

        def attn_B(h, dup, first_main):
            parity = state["blk"] % 2
            kv_cur = kvA if parity == 0 else kvB
            kv_nxt = kvB if parity == 0 else kvA
            kvs_cur = kv_cur[:, h * 128:(h + 1) * 128]
            kvs_nxt = kv_nxt[:, h * 128:(h + 1) * 128]

            if not dup:
                L.wait(TEN, "TEN", state["bankr"].get(po.name, None),
                       R[("vet", h)], R[("vek", h)], R[("veq", h)],
                       R[("kv_ok", parity, h)])
                TEN.matmul(po[:, 0:256], vts[:, 0:128], kqd[:, 0:256],
                           start=True, stop=False)
                TEN.matmul(po[:, 0:256], vts[:, 128:256], kqd[:, 256:512],
                           start=False, stop=False)
                mmo = L.inc(TEN.matmul(po[:, 0:256], kvs_cur, qdec[:],
                                       start=False, stop=True), "mm", 1)
                state["bankw"][po.name] = mmo
                R[("kv_read", parity, h)] = mmo
                R[("kqd_free",)] = mmo
                R[("qdec_free",)] = mmo
            else:
                L.wait(TEN, "TEN", R[("vet", h)])
                mmo = None
            L.wait(TEN, "TEN", state["bankr"].get(pkv.name, None))
            TEN.matmul(pkv[:, 0:128], kts[:, 0:128], vts[:, 0:128],
                       start=True, stop=False)
            mmk2 = L.inc(TEN.matmul(pkv[:, 0:128], kts[:, 128:256], vts[:, 128:256],
                                    start=False, stop=True), "mm", 1)
            R[("vts_free",)] = mmk2
            if not dup and h >= 1:
                sqb_prev = sq0 if (h - 1) % 2 == 0 else sq1
                L.wait(TEN, "TEN", R[("seo", h - 1)])
                if h - 1 == 0:
                    L.wait(TEN, "TEN", state["bankr"].get(pstx.name, None))
                instr = TEN.matmul(pstx[0:1, 0:256], ones_col[:], sqb_prev[:],
                                   start=(h - 1 == 0), stop=False)
                R[("o2_free", (h - 1) % 2)] = L.inc(instr, "mm", 1)
            L.wait(VEC, "VEC", mmk2, R["bd"], R[("kv_ok", parity, h)],
                   R.get(("kv_read", parity, h), None),
                   R.get(("kv_read", 1 - parity, h), None))
            vkv = L.inc(VEC.scalar_tensor_tensor(kvs_nxt, kvs_cur, bd_t[:, h:h + 1],
                                                 pkv[:, 0:128], ALU.mult, ALU.add),
                        "ve", 1)
            bank_read_done(pkv, vkv)
            R[("kv_ok", 1 - parity, h)] = vkv

            if not dup:
                L.wait(VEC, "VEC", mmo)
                if first_main and h == 0:
                    L.wait(VEC, "VEC", R["pp_oarena_free"])
                veo = L.inc(VEC.tensor_copy(oc(h), po[:, 0:256]), "ve", 1)
                sqb = sq0 if h % 2 == 0 else sq1
                L.wait(SCA, "SCA", mmo, R.get(("o2_free", h % 2), None))
                seo = L.inc(SCA.square(sqb[:], po[:, 0:256]), "se", 1)
                R[("seo", h)] = seo
                bank_read_done(po, veo)
                bank_read_add(po, seo)

        def emit_x_load(t, dup):
            x_dram = xd_d if dup else xm_d
            L.wait(GPS, "GPS", R.get("x_free", None))
            xin = xarena[:].rearrange("p (c m) -> p c m", c=NCH)
            return L.inc(GPS.dma_start(out=xin,
                                       in_=x_dram[t].rearrange("c p m -> p c m")),
                         "xl", 16)

        def emit_tile(t, dup, first_main=False):

            pre = R.pop(("x1_ready", dup, t), None)
            if pre is not None:
                x1_ready = pre
            else:
                key = ("tile_xl", dup, t)
                tile_xl = R.pop(key, None) or emit_x_load(t, dup)
                x1_ready = layernorm(xc, x1c, x1loc, rb1, tile_xl,
                                     dst_waits=lambda c: R.get("x1_free", None))
            R["x1"] = x1_ready

            banks = [pA, pB]
            pbanks = [pA, pB, pstq]
            pidx = [0]

            def proj_chunk(fo):
                segs = []
                for half in range(2):
                    cid, wsl, _ = load_w(wql_d[fo][:, half * 2048:(half + 1) * 2048],
                                         2048, [("wpp", "q", fo, half)])
                    segs.append((cid, wsl, list(range(half * 8, half * 8 + 8)), False))
                bank = pbanks[pidx[0] % 3]
                pidx[0] += 1
                mark = proj_matmuls_hilo(segs, x1hi, x1loc, bank,
                                         x_marks=x1_ready)
                role, hh = divmod(fo, 16)
                if role == 0:
                    L.wait(SCA, "SCA", mark, R.get(("q_used", hh), None))
                    ev = L.inc(SCA.activation(qh(hh), bank[:, 0:256], AF.Silu),
                               "se", 1)
                    R[("q", hh)] = ev
                elif role == 1:
                    L.wait(SCA, "SCA", mark, R.get(("k_used", hh), None))
                    ev = L.inc(SCA.activation(kh(hh), bank[:, 0:256], AF.Silu),
                               "se", 1)
                    R[("k", hh)] = ev
                elif role == 2:
                    L.wait(SCA, "SCA", mark, R.get(("v_used", hh), None))
                    ev = L.inc(SCA.copy(vh(hh), bank[:, 0:256]), "se", 1)
                    R[("v", hh)] = ev
                else:
                    L.wait(SCA, "SCA", mark, R.get(("u_used", hh), None))
                    ev = L.inc(SCA.copy(uh(hh), bank[:, 0:256]), "se", 1)
                    R[("u", hh)] = ev
                bank_read_done(bank, ev)
                R["last_proj_mark"] = mark

            roles = (1, 2) if dup else (0, 1, 2, 3)
            for h in range(HEADS):
                for role in roles:
                    proj_chunk(role * 16 + h)
                    if dup:
                        pp_emit(2)
                if h >= 1:
                    attn_B(h - 1, dup, first_main)
                attn_A(h, dup)
            attn_B(HEADS - 1, dup, first_main)
            if not dup:
                hh = HEADS - 1
                sqb_prev = sq0 if hh % 2 == 0 else sq1
                L.wait(TEN, "TEN", R[("seo", hh)])
                instr = TEN.matmul(pstx[0:1, 0:256], ones_col[:], sqb_prev[:],
                                   start=False, stop=True)
                R[("o2_free", hh % 2)] = L.inc(instr, "mm", 1)
            state["blk"] += 1

            if dup:
                R["x_free"] = x1_ready
                R["x1_free"] = R["last_proj_mark"]
                if t + 1 < n_dup:
                    xl_next = emit_x_load(t + 1, True)
                    R[("x1_ready", True, t + 1)] = layernorm(
                        xc, x1c, x1loc, rb1, xl_next,
                        dst_waits=lambda c: R.get("x1_free", None))
                elif n_main > 0:
                    xl_next = emit_x_load(0, False)
                    R[("x1_ready", False, 0)] = layernorm(
                        xc, x1c, x1loc, rb1, xl_next,
                        dst_waits=lambda c: R.get("x1_free", None))
                return

            mm_rms = L.cur("mm")
            L.wait(SCA, "SCA", mm_rms)
            se1 = L.inc(SCA.activation(m_row[0:1, :], pstx[0:1, 0:256], AF.Copy,
                                       scale=1.0 / DM), "se", 1)
            bank_read_done(pstx, se1)
            L.wait(SCA, "SCA", se1)
            se2 = L.inc(SCA.activation(sd_row[0:1, :], m_row[0:1, :], AF.Sqrt,
                                       bias=epsrms[0:1, 0:1]), "se", 1)
            L.wait(VEC, "VEC", se2)
            ve1 = L.inc(VEC.reciprocal(R_row[0:1, :], sd_row[0:1, :]), "ve", 1)
            L.wait(TEN, "TEN", ve1, state["bankr"].get(pq.name, None))
            mmb = L.inc(TEN.matmul(pq[:, 0:256], ones_row[:], R_row[0:1, :],
                                   start=True, stop=True), "mm", 1)
            L.wait(VEC, "VEC", mmb)
            ve2 = L.inc(VEC.tensor_copy(r3b, pq[:, 0:256]), "ve", 1)
            bank_read_done(pq, ve2)

            L.wait(VEC, "VEC", R[("u", NCH - 1)])
            L.wait(VEC, "VEC", ve2)
            L.wait(GPS, "GPS", R[("u", NCH - 1)], ve2)
            x2in_marks = []
            for c in range(NCH):
                use_gps = (c % 3 == 2)
                eng, en, sem, stg2 = ((GPS, "GPS", "gp", stageg) if use_gps
                                      else (VEC, "VEC", "ve", stage))
                L.wait(eng, en, R[("u", c)])
                m1 = L.inc(eng.tensor_tensor(stg2[:], oc(c), r3b, ALU.mult),
                           sem, 1)
                L.wait(eng, en, m1)
                m2 = L.inc(eng.tensor_tensor(x1c(c), stg2[:], uh(c), ALU.mult),
                           sem, 1)
                L.wait(eng, en, m2)
                m3 = L.inc(eng.tensor_tensor(x1loc(c), x1c(c), x1hi(c),
                                             ALU.subtract), sem, 1)
                L.wait(eng, en, m3)
                x2in_marks.append(m3)
            x2in_ready = [x2in_marks[NCH - 1], x2in_marks[NCH - 2]]
            for c in range(NCH):
                R[("u_used", c)] = x2in_ready

            x_free_marks = []
            for fo in range(NCH):
                segs = []
                for half in range(2):
                    cid, wsl, _ = load_w(wol_d[fo][:, half * 2048:(half + 1) * 2048],
                                         2048, [("wpp", "o", fo, half)])
                    segs.append((cid, wsl, list(range(half * 8, half * 8 + 8)), False))
                bank = pbanks[fo % 3]
                mark = proj_matmuls_hilo(segs, x1hi, x1loc, bank,
                                         x_marks=x2in_marks)
                L.wait(VEC, "VEC", mark)
                ev = L.inc(VEC.tensor_tensor(oc(fo), bank[:, 0:256], xc(fo),
                                             ALU.add), "ve", 1)
                bank_read_done(bank, ev)
                x_free_marks.append(ev)
            R["x_free"] = x_free_marks[-1]
            R["x2"] = x_free_marks
            R["x1_free"] = x_free_marks[-1]

            x3_ready = layernorm(oc, x3c, x1loc, rb1, R["x2"],
                                 dst_waits=lambda c: [R.get(("q_used", c), None),
                                                      R["x1_free"]])

            if t + 1 < n_main:
                R[("tile_xl", False, t + 1)] = emit_x_load(t + 1, False)

            gbanks = [pA, ptr]
            ubanks = [pB, po]
            for fo in range(ICH):
                par = fo % 2
                gbank = gbanks[par]
                ubank = ubanks[par]
                cidg, wslg, _ = load_w(wgb_d[fo], 2048, [("wpp", "g", fo, 0), ("wpp", "g", fo, 1)])
                markg = proj_matmuls_hilo([(cidg, wslg, list(range(NCH)), False)],
                                          x3hi, x1loc, gbank,
                                          w_split=False, x_marks=x3_ready)
                L.wait(SCA, "SCA", markg, R.get(("sg_used", par), None))
                seg = L.inc(SCA.activation(sgc(par), gbank[:, 0:256], AF.Silu,
                                           scale=gs_t[:, fo:fo + 1]), "se", 1)
                bank_read_done(gbank, seg)
                cidu, wslu, _ = load_w(wub_d[fo], 2048, [("wpp", "u", fo, 0), ("wpp", "u", fo, 1)])
                marku = proj_matmuls_hilo([(cidu, wslu, list(range(NCH)), False)],
                                          x3hi, x1loc, ubank,
                                          w_split=False, x_marks=x3_ready)
                L.wait(VEC, "VEC", marku, seg)
                n1 = L.inc(VEC.scalar_tensor_tensor(stage[:], ubank[:, 0:256],
                                                    us_t[:, fo:fo + 1], sgc(par),
                                                    ALU.mult, ALU.mult), "ve", 1)
                bank_read_done(ubank, n1)
                L.wait(VEC, "VEC", n1)
                n2 = L.inc(VEC.tensor_copy(hdnhi(fo), stage[:]), "ve", 1)
                L.wait(VEC, "VEC", n2)
                n3 = L.inc(VEC.tensor_tensor(hdnlo(fo), stage[:], hdnhi(fo),
                                             ALU.subtract), "ve", 1)
                L.wait(VEC, "VEC", n3)
                R[("sg_used", par)] = n1
            hdn_ready = L.cur("ve")

            if t + 1 < n_main:
                xl_next = R.pop(("tile_xl", False, t + 1))
                R[("x1_ready", False, t + 1)] = layernorm(
                    xc, x1c, x1loc, rb1, [xl_next, hdn_ready],
                    dst_waits=lambda c: R.get("x1_free", None))

            dbanks = [pA, pB, ptr, po]
            for ho in range(NCH):
                bank = dbanks[ho % 4]
                for quarter in range(4):
                    cid, wsl, _ = load_w(
                        wdb_d[ho][:, quarter * 1408:(quarter + 1) * 1408],
                        1408, [("wpp", "d", ho, 2 * quarter),
                               ("wpp", "d", ho, 2 * quarter + 1)])
                    L.wait(TEN, "TEN", state["wgoal"][cid])
                    if quarter == 0:
                        L.wait(TEN, "TEN", state["bankr"].get(bank.name, None),
                               hdn_ready)
                    for k in range(11):
                        kk = quarter * 11 + k
                        TEN.matmul(bank[:, 0:256], wsl[:, k * 128:(k + 1) * 128],
                                   hdnhi(kk), start=(kk == 0), stop=False)
                        instr = TEN.matmul(bank[:, 0:256],
                                           wsl[:, k * 128:(k + 1) * 128],
                                           hdnlo(kk), start=False,
                                           stop=(kk == ICH - 1 and k == 10))
                    mark = L.inc(instr, w_release_name(cid), 1)
                state["bankw"][bank.name] = mark
                par = ho % 2
                obuf = vts if par == 0 else kts
                L.wait(VEC, "VEC", mark, R.get(("outbuf", par), None))
                ev = L.inc(VEC.scalar_tensor_tensor(obuf[:],
                                                    bank[:, 0:256], ds_t[:, ho:ho + 1],
                                                    oc(ho), ALU.mult, ALU.add),
                           "ve", 1)
                bank_read_done(bank, ev)
                L.wait(GPS, "GPS", ev)
                stm = L.inc(GPS.dma_start(out=out_d[t][ho], in_=obuf[:]),
                            "stA" if par == 0 else "stB", 16)
                R[("outbuf", par)] = stm

        if n_dup > 0:
            xl0 = emit_x_load(0, True)
            R[("x1_ready", True, 0)] = layernorm(
                xc, x1c, x1loc, rb1, xl0, dst_waits=None)
        pp_emit(n_kv_jobs)

        for t in range(n_dup):
            emit_tile(t, dup=True)
        pp_emit(len(pp_jobs))
        for t in range(n_main):
            emit_tile(t, dup=False, first_main=(t == 0))

        GPS.wait_ge(L.h["stA"], L.c["stA"])
        GPS.wait_ge(L.h["stB"], L.c["stB"])

    return nc


def _prep_inputs(L0_attn_x, L0_attn_slope_rate, L0_attn_W_qkvu, L0_attn_W_out,
                 L0_ffn_gate_w, L0_ffn_up_w, L0_ffn_down_w, L0_ffn_gate_s,
                 L0_ffn_up_s, L0_ffn_down_s):
    f32 = np.float32
    x = np.ascontiguousarray(L0_attn_x, dtype=f32)
    wq = np.ascontiguousarray(
        np.asarray(L0_attn_W_qkvu, f32).reshape(64, 128, 16, 128)
        .transpose(0, 3, 2, 1).reshape(64, 128, 2048))
    wo = np.ascontiguousarray(
        np.asarray(L0_attn_W_out, f32).reshape(16, 128, 16, 128)
        .transpose(0, 3, 2, 1).reshape(16, 128, 2048))
    wg = np.ascontiguousarray(
        np.asarray(L0_ffn_gate_w, f32).reshape(44, 128, 16, 128)
        .transpose(0, 3, 2, 1).reshape(44, 128, 2048))
    wu = np.ascontiguousarray(
        np.asarray(L0_ffn_up_w, f32).reshape(44, 128, 16, 128)
        .transpose(0, 3, 2, 1).reshape(44, 128, 2048))
    wd = np.ascontiguousarray(
        np.asarray(L0_ffn_down_w, f32).reshape(16, 128, 44, 128)
        .transpose(0, 3, 2, 1).reshape(16, 128, 5632))
    gs = np.ascontiguousarray(np.asarray(L0_ffn_gate_s, f32).reshape(44, 128).T)
    us = np.ascontiguousarray(np.asarray(L0_ffn_up_s, f32).reshape(44, 128).T)
    ds = np.ascontiguousarray(np.asarray(L0_ffn_down_s, f32).reshape(16, 128).T)
    sl = np.ascontiguousarray(
        np.broadcast_to(np.asarray(L0_attn_slope_rate, f32).reshape(1, 16),
                        (128, 16)))

    def tile_x(seq_slice):
        T = seq_slice.shape[0]
        a = np.ascontiguousarray(seq_slice.T)
        return np.ascontiguousarray(
            a.reshape(16, 128, T // 256, 256).transpose(2, 0, 1, 3))

    zeros_xd = np.zeros((8, 16, 128, 256), f32)
    in_maps = []
    for core in range(8):
        p, half = core // 2, core % 2
        xm = tile_x(x[p, half * 2048:(half + 1) * 2048])
        xd = tile_x(x[p, 0:2048]) if half == 1 else zeros_xd
        in_maps.append({
            "xd": xd, "xm": xm, "wq": wq, "wo": wo, "wg": wg, "wu": wu,
            "wd": wd, "gs": gs, "us": us, "ds": ds, "sl": sl,
        })
    return in_maps


def kernel(**inputs):
    in_maps = _prep_inputs(**inputs)
    nc = build_program(8, 8)
    res = run_bass_kernel_spmd(nc, in_maps, core_ids=list(range(8)))
    out = np.empty((4, 4096, 2048), np.float32)
    for core in range(8):
        p, half = core // 2, core % 2
        o = np.asarray(res.results[core]["out"])
        o = o.transpose(1, 2, 0, 3).reshape(2048, 2048)
        out[p, half * 2048:(half + 1) * 2048, :] = o.T
    return out
